# version 03 (final)
# baseline (speedup 1.0000x reference)
import numpy as np
import ml_dtypes
from contextlib import ExitStack

import concourse.bass as bass
import concourse.mybir as mybir
from concourse.bass import IndirectOffsetOnAxis
from concourse.bass_utils import run_bass_kernel_spmd

F32 = mybir.dt.float32
BF16 = mybir.dt.bfloat16
I32 = mybir.dt.int32
AF = mybir.ActivationFunctionType
ALU = mybir.AluOpType
AX = mybir.AxisListType

NCORES = 8
TPC = 1024
DM = 2048
NDC = DM // 128
SEQ = 4096

COMPUTE_ENGINES = ("tensor", "vector", "scalar", "gpsimd")
ENGINES = COMPUTE_ENGINES + ("sync",)


class Sem:
    def __init__(self, prog, name):
        self.h = prog.ctx.enter_context(prog.nc.semaphore(name))
        self.n = 0


class Prog:

    def __init__(self, nc, ctx):
        self.nc = nc
        self.ctx = ctx
        self.qs = {k: [] for k in ENGINES}
        self.all_sems = []
        self.e = {k: self.sem(f"E_{k}") for k in COMPUTE_ENGINES}

    def sem(self, name):
        s = Sem(self, name)
        self.all_sems.append(s)
        return s

    def coll(self, kind, ins, outs, sem, groups=None):
        sem.n += 1
        h = sem.h
        rg = groups if groups is not None else [list(range(NCORES))]
        self.qs["gpsimd"].append(lambda e: e.collective_compute(
            kind, ALU.bypass, replica_groups=rg, ins=ins, outs=outs).then_inc(h, 1))
        return sem.n

    def barrier(self):
        for eng in ENGINES:
            for s in self.all_sems:
                if s.n > 0:
                    self.wait(eng, s, s.n)

    def sbuf(self, name, shape, dtype=F32):
        return self.ctx.enter_context(self.nc.sbuf_tensor(name, shape, dtype))

    def psum(self, name, shape, dtype=F32):
        return self.ctx.enter_context(self.nc.psum_tensor(name, shape, dtype))

    def op(self, eng, fn, chain=True):
        s = self.e[eng]
        if chain and s.n > 0:
            self.wait(eng, s, s.n)
        s.n += 1
        h, v = s.h, s.n
        self.qs[eng].append(lambda e: fn(e).then_inc(h, 1))
        return s.n

    def T(self, fn, chain=True):
        return self.op("tensor", fn, chain)

    def V(self, fn, chain=True):
        return self.op("vector", fn, chain)

    def S(self, fn, chain=True):
        return self.op("scalar", fn, chain)

    def G(self, fn, chain=True):
        return self.op("gpsimd", fn, chain)

    def wait(self, eng, sem, n=None):
        v = sem.n if n is None else n
        if v <= 0:
            return
        h = sem.h
        self.qs[eng].append(lambda e: e.wait_ge(h, v))

    def dma(self, q, out, in_, sem, serial=False):
        if serial and sem.n > 0:
            self.wait(q, sem, sem.n)
        sem.n += 16
        h = sem.h
        self.qs[q].append(lambda e: e.dma_start(out=out, in_=in_).then_inc(h, 16))
        return sem.n

    def dma_ex(self, q, out, in_, sem, accum=False):
        sem.n += 16
        h = sem.h
        if accum:
            self.qs[q].append(lambda e: e.dma_start(out=out, in_=in_, accum_op=ALU.add).then_inc(h, 16))
        else:
            self.qs[q].append(lambda e: e.dma_start(out=out, in_=in_).then_inc(h, 16))
        return sem.n

    def idma(self, out, in_, idx_ap, axis, sem):
        sem.n += 16
        h = sem.h
        self.qs["gpsimd"].append(lambda e: e.indirect_dma_start(
            out=out, out_offset=None, in_=in_,
            in_offset=IndirectOffsetOnAxis(idx_ap, axis)).then_inc(h, 16))
        return sem.n

    def finish(self):
        block = self.ctx.enter_context(self.nc.Block())
        for eng in ENGINES:
            fns = self.qs[eng]
            if fns:
                getattr(block, eng)(lambda e, fns=fns: [f(e) for f in fns])


class Arena:

    def __init__(self, nc):
        self.nc = nc
        self.base = nc.sbuf_base
        self.cap = nc.sbuf_top
        self.off = self.base
        self.n = 0

    def start(self, keep=0):
        self.off = self.base + keep

    def __call__(self, name, shape, dtype=F32):
        per = int(np.prod(shape[1:])) * mybir.dt.size(dtype)
        off = (self.off + 31) // 32 * 32
        self.off = off + per
        assert self.off <= self.cap, f"arena overflow at {name}: {self.off} > {self.cap}"
        self.n += 1
        return self.nc.alloc_sbuf_tensor_at(f"{name}_{self.n}", list(shape), dtype, offset=off)

    def mark(self):
        return self.off - self.base


class Ring:

    def __init__(self, p, name, R):
        self.sems = [Sem(p, f"{name}{r}") for r in range(R)]
        self.R = R

    def __getitem__(self, i):
        return self.sems[i % self.R]


def fm_dram_ap(dram_ap, nchunks):
    return dram_ap.rearrange("(c p) t -> p c t", p=128)


def fm_load(p, sb, dram_ap, nchunks, sem, q="sync"):
    return p.dma(q, sb[:].rearrange("p (c t) -> p c t", c=nchunks), fm_dram_ap(dram_ap, nchunks), sem)


def build_ln(nadd=0, emit_sum=False):
    nc = bass.Bass("TRN2", target_bir_lowering=False)
    x_d = nc.dram_tensor("x", [TPC, DM], F32, kind="ExternalInput").ap()
    adds = [nc.dram_tensor(f"add{j}", [TPC, DM], F32, kind="ExternalInput").ap() for j in range(nadd)]
    xn_d = nc.dram_tensor("xn", [TPC, DM], F32, kind="ExternalOutput").ap()
    xs_d = nc.dram_tensor("xsum", [TPC, DM], F32, kind="ExternalOutput").ap() if emit_sum else None
    ntiles = TPC // 128
    ctx = ExitStack()
    with ctx:
        p = Prog(nc, ctx)
        eV, eS = p.e["vector"], p.e["scalar"]
        xt = [p.sbuf(f"xt{i}", [128, DM]) for i in range(2)]
        at = [p.sbuf(f"at{i}", [128, DM]) for i in range(2)]
        ot = [p.sbuf(f"ot{i}", [128, DM]) for i in range(2)]
        junk = p.sbuf("junk", [128, DM])
        s1 = p.sbuf("s1", [128, ntiles])
        s2 = p.sbuf("s2", [128, ntiles])
        mean = p.sbuf("mean", [128, ntiles])
        var = p.sbuf("var", [128, ntiles])
        rstd = p.sbuf("rstd", [128, ntiles])
        cinv = p.sbuf("cinv", [128, 1])
        ldr = Ring(p, "ld", 2)
        str_ = Ring(p, "st", 2)

        cv = p.V(lambda e: e.memset(cinv[:], 1.0 / DM))
        store_done = []

        for i in range(ntiles):
            b = i % 2
            xs = xt[b]
            lds = ldr[i]
            sts = str_[i]
            rows = slice(i * 128, (i + 1) * 128)
            if i >= 2:
                p.wait("sync", *store_done[i - 2])
            ldn = p.dma("sync", xs[:], x_d[rows, :], lds)
            for aj, ad in enumerate(adds):
                abuf = at[b]
                if aj > 0:
                    p.wait("sync", eV, eV.n)
                ldn = p.dma("sync", abuf[:], ad[rows, :], lds)
                p.wait("vector", lds, ldn)
                p.V(lambda e, xs=xs, abuf=abuf: e.tensor_tensor(xs[:], xs[:], abuf[:], ALU.add))
            if adds:
                xready = eV.n
                p.wait("scalar", eV, xready)
                if emit_sum:
                    p.wait("sync", eV, xready)
                    p.dma("sync", xs_d[rows, :], xs[:], sts)
            else:
                p.wait("vector", lds, ldn)
                p.wait("scalar", lds, ldn)
            si = slice(i, i + 1)
            v_s1 = p.V(lambda e, xs=xs, si=si: e.reduce_sum(s1[:, si], xs[:], axis=AX.X))
            a_s2 = p.S(lambda e, xs=xs, si=si: e.activation(junk[:], xs[:], AF.Square, accum_out=s2[:, si]))
            p.wait("vector", eS, a_s2)
            p.V(lambda e, si=si: e.tensor_scalar(mean[:, si], s1[:, si], cinv[:], None, ALU.mult))
            p.V(lambda e, si=si: e.tensor_tensor(var[:, si], mean[:, si], mean[:, si], ALU.mult))
            p.V(lambda e, si=si: e.scalar_tensor_tensor(var[:, si], s2[:, si], cinv[:], var[:, si], ALU.mult, ALU.subtract))
            v_var = p.V(lambda e, si=si: e.tensor_single_scalar(var[:, si], var[:, si], 1e-5, ALU.add))
            p.wait("scalar", eV, v_var)
            a_sqrt = p.S(lambda e, si=si: e.activation(var[:, si], var[:, si], AF.Sqrt))
            p.wait("vector", eS, a_sqrt)
            p.V(lambda e, si=si: e.reciprocal(rstd[:, si], var[:, si]))
            if i >= 2:
                p.wait("vector", *store_done[i - 2])
            ob = ot[b]
            v_norm = p.V(lambda e, xs=xs, ob=ob, i=i: e.tensor_scalar(
                ob[:], xs[:], mean[:, i:i + 1], rstd[:, i:i + 1], ALU.subtract, ALU.mult))
            p.wait("sync", eV, v_norm)
            p.dma("sync", xn_d[rows, :], ob[:], sts)
            store_done.append((sts, sts.n))
        for s in str_.sems:
            p.wait("sync", s, s.n)
        p.finish()
    return nc


def build_dbc_delta():
    nc = bass.Bass("TRN2", target_bir_lowering=False)
    xnT_d = nc.dram_tensor("xnT", [DM, TPC], F32, kind="ExternalInput").ap()
    winT_d = nc.dram_tensor("W_inT", [DM, 96], F32, kind="ExternalInput").ap()
    wdtT_d = nc.dram_tensor("W_dtT", [64, DM], F32, kind="ExternalInput").ap()
    bdt_d = nc.dram_tensor("b_dt_r", [128, NDC], F32, kind="ExternalInput").ap()
    dbcT_d = nc.dram_tensor("dbcT", [96, TPC], F32, kind="ExternalOutput").ap()
    deltaT_d = nc.dram_tensor("deltaT", [DM, TPC], F32, kind="ExternalOutput").ap()
    ctx = ExitStack()
    with ctx:
        p = Prog(nc, ctx)
        eT, eV, eS = p.e["tensor"], p.e["vector"], p.e["scalar"]
        xn = p.sbuf("xn", [128, NDC * TPC])
        win = p.sbuf("win", [128, NDC * 96])
        wdt = p.sbuf("wdt", [64, DM])
        bdt = p.sbuf("bdt", [128, NDC])
        dbc = p.sbuf("dbc", [96, TPC])
        v_sb = [p.sbuf(f"v_sb{i}", [128, 512]) for i in range(2)]
        r_sb = [p.sbuf(f"r_sb{i}", [128, 512]) for i in range(2)]
        w_sb = [p.sbuf(f"w_sb{i}", [128, 512]) for i in range(2)]
        el_sb = [p.sbuf(f"el_sb{i}", [128, 512]) for i in range(2)]
        dl_sb = [p.sbuf(f"dl_sb{i}", [128, 512]) for i in range(2)]
        one = p.sbuf("one", [128, 1])
        ps1 = p.psum("ps1", [96, 512])
        ps2 = p.psum("ps2", [96, 512])
        psd = [p.psum(f"psd{i}", [128, 512]) for i in range(2)]
        ld = p.sem("ld")
        stb = p.sem("stb")
        str_ = Ring(p, "st", 2)

        p.V(lambda e: e.memset(one[:], 1.0))

        fm_load(p, xn, xnT_d, NDC, ld)
        fm_load(p, win, winT_d, NDC, ld)
        p.dma("sync", wdt[:], wdtT_d[:], ld)
        p.dma("sync", bdt[:], bdt_d[:], ld)
        ldall = ld.n
        p.wait("tensor", ld, ldall)
        p.wait("vector", ld, ldall)
        p.wait("scalar", ld, ldall)

        mm_done = []
        for th in range(2):
            ps = ps1 if th == 0 else ps2
            for c in range(NDC):
                n = p.T(lambda e, ps=ps, c=c, th=th: e.matmul(
                    ps[:, :], win[:, c * 96:(c + 1) * 96],
                    xn[:, c * TPC + th * 512: c * TPC + th * 512 + 512],
                    start=(c == 0), stop=(c == NDC - 1)))
            mm_done.append(n)
        p.wait("vector", eT, mm_done[0])
        p.V(lambda e: e.tensor_copy(dbc[:, 0:512], ps1[:, :]))
        p.wait("vector", eT, mm_done[1])
        dbc_v = p.V(lambda e: e.tensor_copy(dbc[:, 512:1024], ps2[:, :]))
        p.wait("sync", eV, dbc_v)
        p.dma("sync", dbcT_d[:], dbc[:], stb)
        p.wait("tensor", eV, dbc_v)

        store_done = []
        psum_free = {}
        for c in range(NDC):
            for th in range(2):
                k = c * 2 + th
                par = k % 2
                ps = psd[par]
                if k >= 2:
                    p.wait("tensor", eV, psum_free[k - 2])
                mmn = p.T(lambda e, ps=ps, c=c, th=th: e.matmul(
                    ps[:, :], wdt[:, c * 128:(c + 1) * 128],
                    dbc[0:64, th * 512:(th + 1) * 512], start=True, stop=True))
                p.wait("vector", eT, mmn)
                p.V(lambda e, ps=ps, c=c, par=par: e.tensor_scalar(
                    v_sb[par][:], ps[:, :], bdt[:, c:c + 1], None, ALU.add))
                psum_free[k] = p.V(lambda e, par=par: e.tensor_single_scalar(
                    r_sb[par][:], v_sb[par][:], 0.0, ALU.max))
                wv = p.V(lambda e, par=par: e.scalar_tensor_tensor(
                    w_sb[par][:], r_sb[par][:], -2.0, v_sb[par][:], ALU.mult, ALU.add))
                p.wait("scalar", eV, wv)
                p.S(lambda e, par=par: e.activation(el_sb[par][:], w_sb[par][:], AF.Exp))
                lnn = p.S(lambda e, par=par: e.activation(el_sb[par][:], el_sb[par][:], AF.Ln, bias=one[:]))
                if k >= 2:
                    p.wait("vector", *store_done[k - 2])
                p.wait("vector", eS, lnn)
                dl = p.V(lambda e, par=par: e.tensor_tensor(dl_sb[par][:], r_sb[par][:], el_sb[par][:], ALU.add))
                p.wait("sync", eV, dl)
                sts = str_[k]
                p.dma("sync", deltaT_d[c * 128:(c + 1) * 128, th * 512:(th + 1) * 512], dl_sb[par][:], sts)
                store_done.append((sts, sts.n))
        p.wait("sync", stb, stb.n)
        for s in str_.sems:
            p.wait("sync", s, s.n)
        p.finish()
    return nc


def build_scan():
    TT = NCORES * TPC
    nc = bass.Bass("TRN2", target_bir_lowering=False)
    deltaT_d = nc.dram_tensor("deltaT_sl", [256, TT], F32, kind="ExternalInput").ap()
    xnT_d = nc.dram_tensor("xnT_sl", [256, TT], F32, kind="ExternalInput").ap()
    bt_d = nc.dram_tensor("Bt", [16, TT], F32, kind="ExternalInput").ap()
    ct_d = nc.dram_tensor("Ct", [16, TT], F32, kind="ExternalInput").ap()
    alog_d = nc.dram_tensor("A_log_r", [128, 32], F32, kind="ExternalInput").ap()
    dr_d = nc.dram_tensor("D_r", [128, 2], F32, kind="ExternalInput").ap()
    y_d = nc.dram_tensor("yT_sl", [256, TT], F32, kind="ExternalOutput").ap()
    ctx = ExitStack()
    with ctx:
        p = Prog(nc, ctx)
        eT, eV, eS = p.e["tensor"], p.e["vector"], p.e["scalar"]
        delta = [p.sbuf("delta0", [128, SEQ])]
        dxq = [p.sbuf("dxq0", [128, SEQ])]
        yq = [p.sbuf("yq0", [128, SEQ])]
        bt = p.sbuf("bt", [16, SEQ])
        ct = p.sbuf("ct", [16, SEQ])
        acol = p.sbuf("acol", [128, 32])
        dcol = p.sbuf("dcol", [128, 2])
        ones1 = p.sbuf("ones1", [1, 128])
        esel = p.sbuf("esel", [16, DM])
        tsel = p.sbuf("tsel", [16, DM])
        bbc = p.sbuf("bbc", [128, SEQ])
        cbc = p.sbuf("cbc", [128, SEQ])
        a_t = p.sbuf("a_t", [128, SEQ])
        b_t = p.sbuf("b_t", [128, SEQ])
        h_t = p.sbuf("h_t", [128, SEQ])
        tmp = p.sbuf("tmp", [128, SEQ])
        psb = p.psum("psb", [128, 512])
        psc = p.psum("psc", [128, 512])
        ld0 = p.sem("ld0")
        ldq = Ring(p, "ldq", 2)
        str_ = Ring(p, "st", 2)

        p.G(lambda e: e.memset(ones1[:], 1.0))
        p.G(lambda e: e.iota(esel[:].rearrange("k (n q) -> k n q", q=128),
                             pattern=[[1, 16], [0, 128]], base=0, channel_multiplier=0,
                             allow_small_or_imprecise_dtypes=True))
        g_io = p.G(lambda e: e.iota(tsel[:], pattern=[[0, DM]], base=0, channel_multiplier=1,
                                    allow_small_or_imprecise_dtypes=True))
        p.wait("vector", p.e["gpsimd"], g_io)
        esel_v = p.V(lambda e: e.tensor_tensor(esel[:], esel[:], tsel[:], ALU.is_equal))
        p.wait("tensor", eV, esel_v)
        p.dma("sync", acol[:], alog_d[:], ld0)
        p.dma("sync", dcol[:], dr_d[:], ld0)
        ld0n = ld0.n

        p.wait("scalar", ld0, ld0n)
        a_exp = p.S(lambda e: e.activation(acol[:], acol[:], AF.Exp))
        p.wait("vector", eS, a_exp)
        acol_v = p.V(lambda e: e.tensor_single_scalar(acol[:], acol[:], -1.0, ALU.mult))
        p.wait("tensor", ld0, ld0n)
        p.wait("tensor", p.e["gpsimd"], 1)
        p.wait("scalar", eV, acol_v)
        p.wait("vector", ld0, ld0n)

        store_done = []
        for qi, (Dc, b) in enumerate([(d, b) for d in range(2) for b in range(2)]):
            par = 0
            lds = ldq[qi]
            rows = slice(Dc * 128, (Dc + 1) * 128)
            cols = slice(b * SEQ, (b + 1) * SEQ)
            if qi >= 1:
                p.wait("sync", *store_done[qi - 1])
            p.dma("sync", delta[par][:], deltaT_d[rows, cols], lds)
            p.dma("sync", bt[:], bt_d[:, cols], lds)
            p.dma("sync", ct[:], ct_d[:, cols], lds)
            ldn = p.dma("sync", dxq[par][:], xnT_d[rows, cols], lds)
            p.wait("vector", lds, ldn)
            p.wait("tensor", lds, ldn)
            p.V(lambda e, par=par, Dc=Dc: e.tensor_scalar(
                yq[par][:], dxq[par][:], dcol[:, Dc:Dc + 1], None, ALU.mult))
            dx_v = p.V(lambda e, par=par: e.tensor_tensor(
                dxq[par][:], delta[par][:], dxq[par][:], ALU.mult))
            p.wait("scalar", lds, ldn)
            for n in range(16):
                for j in range(8):
                    sl = slice(j * 512, (j + 1) * 512)
                    dsl = slice(j * 512, (j + 1) * 512)
                    m1 = p.T(lambda e, sl=sl, n=n: e.matmul(
                        psb[:, :], esel[:, n * 128:(n + 1) * 128], bt[:, sl], start=True, stop=True))
                    m2 = p.T(lambda e, sl=sl, n=n: e.matmul(
                        psc[:, :], esel[:, n * 128:(n + 1) * 128], ct[:, sl], start=True, stop=True))
                    p.wait("vector", eT, m2)
                    p.V(lambda e, dsl=dsl: e.tensor_copy(bbc[:, dsl], psb[:, :]))
                    vj = p.V(lambda e, dsl=dsl: e.tensor_copy(cbc[:, dsl], psc[:, :]))
                    p.wait("tensor", eV, vj)
                col = Dc * 16 + n
                p.wait("scalar", eV, eV.n)
                an = p.S(lambda e, par=par, col=col: e.activation(
                    a_t[:], delta[par][:], AF.Exp, scale=acol[:, col:col + 1]))
                p.V(lambda e, par=par: e.tensor_tensor(b_t[:], dxq[par][:], bbc[:], ALU.mult))
                p.wait("vector", eS, an)
                p.V(lambda e: e.tensor_tensor_scan(h_t[:], a_t[:], b_t[:], 0.0, ALU.mult, ALU.add))
                p.V(lambda e: e.tensor_tensor(tmp[:], h_t[:], cbc[:], ALU.mult))
                p.V(lambda e, par=par: e.tensor_tensor(yq[par][:], yq[par][:], tmp[:], ALU.add))
            p.wait("sync", eV, eV.n)
            sts = str_[qi]
            p.dma("sync", y_d[rows, cols], yq[par][:], sts)
            store_done.append((sts, sts.n))
        for s in str_.sems:
            p.wait("sync", s, s.n)
        p.finish()
    return nc





def build_mlp():
    NIC = 64
    nc = bass.Bass("TRN2", target_bir_lowering=False)
    xnT_d = nc.dram_tensor("xnT", [DM, TPC], F32, kind="ExternalInput").ap()
    gate_d = nc.dram_tensor("gate_r", [NIC, NDC, 128, 128], F32, kind="ExternalInput").ap()
    up_d = nc.dram_tensor("up_r", [NIC, NDC, 128, 128], F32, kind="ExternalInput").ap()
    down_d = nc.dram_tensor("down_r", [NDC, NIC, 128, 128], F32, kind="ExternalInput").ap()
    mlpT_d = nc.dram_tensor("mlpT", [DM, TPC], F32, kind="ExternalOutput").ap()
    ctx = ExitStack()
    with ctx:
        p = Prog(nc, ctx)
        eT, eV, eS = p.e["tensor"], p.e["vector"], p.e["scalar"]
        xn_bf = p.sbuf("xn_bf", [128, NDC * TPC], BF16)
        gsl = [p.sbuf(f"gsl{i}", [128, NDC * 128], BF16) for i in range(2)]
        usl = [p.sbuf(f"usl{i}", [128, NDC * 128], BF16) for i in range(2)]
        dsl = [p.sbuf(f"dsl{i}", [128, NIC * 128], BF16) for i in range(2)]
        act = p.sbuf("act", [128, NIC * 512], BF16)
        sg = p.sbuf("sg", [128, 512])
        mo = [p.sbuf(f"mo{i}", [128, 512]) for i in range(2)]
        psg = p.psum("psg", [128, 512])
        psu = p.psum("psu", [128, 512])
        psd = [p.psum(f"psd{i}", [128, 512]) for i in range(2)]
        ldx = p.sem("ldx")
        ldg = Ring(p, "ldg", 2)
        ldd = Ring(p, "ldd", 2)
        str_ = Ring(p, "st", 2)

        p.dma("gpsimd", xn_bf[:].rearrange("p (c t) -> p c t", c=NDC), fm_dram_ap(xnT_d, NDC), ldx)
        p.wait("tensor", ldx, ldx.n)

        gu_done = {}
        down_mm = {}
        store_done = []
        n_store = 0
        for th in range(2):
            tsl = slice(th * 512, (th + 1) * 512)
            for ic in range(NIC):
                gi = th * NIC + ic
                par = gi % 2
                lds = ldg[gi]
                if gi >= 2:
                    p.wait("gpsimd", eV, gu_done[gi - 2])
                p.dma("gpsimd", gsl[par][:].rearrange("p (c i) -> p c i", c=NDC),
                      gate_d[ic].rearrange("c p i -> p c i"), lds)
                ldn = p.dma("gpsimd", usl[par][:].rearrange("p (c i) -> p c i", c=NDC),
                            up_d[ic].rearrange("c p i -> p c i"), lds)
                p.wait("tensor", lds, ldn)
                for c in range(NDC):
                    p.T(lambda e, par=par, c=c, tsl=tsl: e.matmul(
                        psg[:, :], gsl[par][:, c * 128:(c + 1) * 128],
                        xn_bf[:, c * TPC + tsl.start: c * TPC + tsl.stop], start=(c == 0), stop=(c == NDC - 1)))
                mmu = None
                for c in range(NDC):
                    mmu = p.T(lambda e, par=par, c=c, tsl=tsl: e.matmul(
                        psu[:, :], usl[par][:, c * 128:(c + 1) * 128],
                        xn_bf[:, c * TPC + tsl.start: c * TPC + tsl.stop], start=(c == 0), stop=(c == NDC - 1)))
                p.wait("scalar", eT, mmu)
                sgn = p.S(lambda e: e.activation(sg[:], psg[:, :], AF.Silu))
                p.wait("vector", eS, sgn)
                gu_done[gi] = p.V(lambda e, ic=ic: e.tensor_tensor(
                    act[:, ic * 512:(ic + 1) * 512], sg[:], psu[:, :], ALU.mult))
                p.wait("tensor", eV, gu_done[gi])
            for dc in range(NDC):
                gj = th * NDC + dc
                par = gj % 2
                lds = ldd[gj]
                if gj >= 2:
                    p.wait("gpsimd", eT, down_mm[gj - 2])
                ldn = p.dma("gpsimd", dsl[par][:].rearrange("p (c d) -> p c d", c=NIC),
                            down_d[dc].rearrange("c p d -> p c d"), lds)
                p.wait("tensor", lds, ldn)
                mmn = None
                for ic in range(NIC):
                    mmn = p.T(lambda e, par=par, ic=ic, gj=gj: e.matmul(
                        psd[gj % 2][:, :], dsl[par][:, ic * 128:(ic + 1) * 128],
                        act[:, ic * 512:(ic + 1) * 512], start=(ic == 0), stop=(ic == NIC - 1)))
                down_mm[gj] = mmn
                p.wait("vector", eT, mmn)
                if n_store >= 2:
                    p.wait("vector", *store_done[n_store - 2])
                mon = p.V(lambda e, gj=gj: e.tensor_copy(mo[gj % 2][:], psd[gj % 2][:, :]))
                p.wait("tensor", eV, mon)
                p.wait("sync", eV, mon)
                sts = str_[gj]
                p.dma("sync", mlpT_d[dc * 128:(dc + 1) * 128, tsl], mo[gj % 2][:], sts)
                store_done.append((sts, sts.n))
                n_store += 1
        for s in str_.sems:
            p.wait("sync", s, s.n)
        p.finish()
    return nc


def build_qproj():
    nc = bass.Bass("TRN2", target_bir_lowering=False)
    xnT_d = nc.dram_tensor("xnT", [DM, TPC], F32, kind="ExternalInput").ap()
    wq_d = nc.dram_tensor("W_q", [DM, 128], F32, kind="ExternalInput").ap()
    qT_d = nc.dram_tensor("qT", [128, TPC], F32, kind="ExternalOutput").ap()
    ctx = ExitStack()
    with ctx:
        p = Prog(nc, ctx)
        eT, eV = p.e["tensor"], p.e["vector"]
        xn = p.sbuf("xn", [128, NDC * TPC])
        wq = p.sbuf("wq", [128, NDC * 128])
        qo = p.sbuf("qo", [128, TPC])
        ps = [p.psum(f"ps{i}", [128, 512]) for i in range(2)]
        ld = p.sem("ld")
        st = p.sem("st")
        fm_load(p, xn, xnT_d, NDC, ld)
        fm_load(p, wq, wq_d, NDC, ld)
        p.wait("tensor", ld, ld.n)
        for th in range(2):
            mmn = None
            for c in range(NDC):
                mmn = p.T(lambda e, th=th, c=c: e.matmul(
                    ps[th][:, :], wq[:, c * 128:(c + 1) * 128],
                    xn[:, c * TPC + th * 512: c * TPC + th * 512 + 512],
                    start=(c == 0), stop=(c == NDC - 1)))
            p.wait("vector", eT, mmn)
            p.V(lambda e, th=th: e.tensor_copy(qo[:, th * 512:(th + 1) * 512], ps[th][:, :]))
        p.wait("sync", eV, eV.n)
        p.dma("sync", qT_d[:], qo[:], st)
        p.wait("sync", st, st.n)
        p.finish()
    return nc


def build_peer():
    NE = 4096
    nc = bass.Bass("TRN2", target_bir_lowering=False)
    xntok_d = nc.dram_tensor("xn_tok", [TPC, DM], F32, kind="ExternalInput").ap()
    q0_d = nc.dram_tensor("Q0T", [64, TPC], F32, kind="ExternalInput").ap()
    q1_d = nc.dram_tensor("Q1T", [64, TPC], F32, kind="ExternalInput").ap()
    keys_d = nc.dram_tensor("keys_r", [64, 128], F32, kind="ExternalInput").ap()
    de_d = nc.dram_tensor("down_embed", [NE, DM], F32, kind="ExternalInput").ap()
    ue_d = nc.dram_tensor("up_embed", [NE, DM], F32, kind="ExternalInput").ap()
    y_d = nc.dram_tensor("peer_tok", [TPC, DM], F32, kind="ExternalOutput").ap()
    ntiles = TPC // 128
    ctx = ExitStack()
    with ctx:
        p = Prog(nc, ctx)
        eT, eV, eS, eG = p.e["tensor"], p.e["vector"], p.e["scalar"], p.e["gpsimd"]
        q0 = p.sbuf("q0", [64, TPC])
        q1 = p.sbuf("q1", [64, TPC])
        keys = p.sbuf("keys", [64, 128])
        xt = [p.sbuf(f"xt{i}", [128, DM]) for i in range(2)]
        yt = [p.sbuf(f"yt{i}", [128, DM]) for i in range(2)]
        ga = p.sbuf("ga", [128, 8 * DM])
        r0 = p.sbuf("r0", [128, 64])
        r1 = p.sbuf("r1", [128, 64])
        iota64 = p.sbuf("iota64", [128, 64])
        junk64 = p.sbuf("junk64", [128, 64])
        junk2k = p.sbuf("junk2k", [128, DM])
        sx8 = p.sbuf("sx8", [128, 8])
        sy8 = p.sbuf("sy8", [128, 8])
        ix8 = p.sbuf("ix8", [128, 8])
        iy8 = p.sbuf("iy8", [128, 8])
        all_s = p.sbuf("all_s", [128, 64])
        all_i = p.sbuf("all_i", [128, 64])
        sc8 = p.sbuf("sc8", [128, 8])
        fidx = p.sbuf("fidx", [128, 8])
        idx32 = p.sbuf("idx32", [128, 8], I32)
        negmax = p.sbuf("negmax", [128, 1])
        pexp = p.sbuf("pexp", [128, 8])
        ssum = p.sbuf("ssum", [128, 1])
        rsum = p.sbuf("rsum", [128, 1])
        ew8 = p.sbuf("ew8", [128, 8])
        sil8 = p.sbuf("sil8", [128, 8])
        w8 = p.sbuf("w8", [128, 8])
        psr = p.psum("psr", [128, 512])
        ld0 = p.sem("ld0")
        ldx = Ring(p, "ldx", 2)
        gdma = Ring(p, "gdma", 2)
        str_ = Ring(p, "st", 2)

        p.G(lambda e: e.iota(iota64[:], pattern=[[1, 64]], base=0, channel_multiplier=0,
                             allow_small_or_imprecise_dtypes=True))
        p.dma("sync", q0[:], q0_d[:], ld0)
        p.dma("sync", q1[:], q1_d[:], ld0)
        p.dma("sync", keys[:], keys_d[:], ld0)
        ld0n = ld0.n
        p.wait("tensor", ld0, ld0n)
        p.wait("vector", p.e["gpsimd"], 1)

        store_done = []
        for i in range(ntiles):
            par = i % 2
            lds = ldx[i]
            rows = slice(i * 128, (i + 1) * 128)
            msl = slice(i * 128, (i + 1) * 128)
            if i >= 2:
                p.wait("sync", *store_done[i - 2])
            ldn = p.dma("sync", xt[par][:], xntok_d[rows, :], lds)
            m0 = p.T(lambda e, msl=msl: e.matmul(psr[:, 0:64], q0[:, msl], keys[:, 0:64], start=True, stop=True))
            m1 = p.T(lambda e, msl=msl: e.matmul(psr[:, 64:128], q1[:, msl], keys[:, 64:128], start=True, stop=True))
            p.wait("vector", eT, m1)
            p.V(lambda e: e.tensor_copy(r0[:], psr[:, 0:64]))
            ev = p.V(lambda e: e.tensor_copy(r1[:], psr[:, 64:128]))
            p.wait("tensor", eV, ev)
            p.V(lambda e: e.max(out=sx8[:], in_=r0[:]))
            for c in range(8):
                p.V(lambda e, c=c: e.scalar_tensor_tensor(
                    junk64[:], r0[:], sx8[:, c:c + 1], iota64[:], ALU.is_equal, ALU.mult,
                    accum_out=ix8[:, c:c + 1]))
            p.V(lambda e: e.max(out=sy8[:], in_=r1[:]))
            for c in range(8):
                p.V(lambda e, c=c: e.scalar_tensor_tensor(
                    junk64[:], r1[:], sy8[:, c:c + 1], iota64[:], ALU.is_equal, ALU.mult,
                    accum_out=iy8[:, c:c + 1]))
            p.V(lambda e: e.tensor_tensor(
                all_s[:].rearrange("p (a c) -> p a c", c=8),
                sx8[:].rearrange("p (k o) -> p k o", o=1).to_broadcast([128, 8, 8]),
                sy8[:].rearrange("p (o k) -> p o k", o=1).to_broadcast([128, 8, 8]),
                ALU.add))
            p.V(lambda e: e.tensor_single_scalar(ix8[:], ix8[:], 64.0, ALU.mult))
            p.V(lambda e: e.tensor_tensor(
                all_i[:].rearrange("p (a c) -> p a c", c=8),
                ix8[:].rearrange("p (k o) -> p k o", o=1).to_broadcast([128, 8, 8]),
                iy8[:].rearrange("p (o k) -> p o k", o=1).to_broadcast([128, 8, 8]),
                ALU.add))
            p.V(lambda e: e.max(out=sc8[:], in_=all_s[:]))
            for c in range(8):
                p.V(lambda e, c=c: e.scalar_tensor_tensor(
                    junk64[:], all_s[:], sc8[:, c:c + 1], all_i[:], ALU.is_equal, ALU.mult,
                    accum_out=fidx[:, c:c + 1]))
            p.V(lambda e: e.tensor_single_scalar(fidx[:], fidx[:], 4095.0, ALU.min))
            p.V(lambda e: e.tensor_single_scalar(negmax[:], sc8[:, 0:1], -1.0, ALU.mult))
            vneg = eV.n
            p.wait("scalar", eV, vneg)
            aexp = p.S(lambda e: e.activation(pexp[:], sc8[:], AF.Exp, bias=negmax[:], accum_out=ssum[:]))
            p.wait("vector", eS, aexp)
            p.V(lambda e: e.reciprocal(rsum[:], ssum[:]))
            vidx = p.V(lambda e: e.tensor_copy(idx32[:], fidx[:]))
            p.wait("gpsimd", eV, vidx)
            gn = None
            for k in range(8):
                gn = p.idma(ga[:, k * DM:(k + 1) * DM], de_d[:, :],
                            idx32[:, k:k + 1], 0, gdma[i])
            p.wait("vector", lds, ldn)
            p.wait("vector", gdma[i], gn)
            for k in range(8):
                p.V(lambda e, k=k, par=par: e.scalar_tensor_tensor(
                    junk2k[:], ga[:, k * DM:(k + 1) * DM], 0.0, xt[par][:], ALU.bypass, ALU.mult,
                    accum_out=ew8[:, k:k + 1]))
            dots_v = eV.n
            p.wait("scalar", eV, dots_v)
            asil = p.S(lambda e: e.activation(sil8[:], ew8[:], AF.Silu))
            p.wait("vector", eS, asil)
            p.V(lambda e: e.scalar_tensor_tensor(w8[:], sil8[:], rsum[:], pexp[:], ALU.mult, ALU.mult))
            w8_v = eV.n
            p.wait("gpsimd", eV, dots_v)
            gn2 = None
            for k in range(8):
                gn2 = p.idma(ga[:, k * DM:(k + 1) * DM], ue_d[:, :],
                             idx32[:, k:k + 1], 0, gdma[i])
            p.wait("vector", gdma[i], gn2)
            if i >= 2:
                p.wait("vector", *store_done[i - 2])
            p.V(lambda e, par=par: e.tensor_scalar(
                yt[par][:], ga[:, 0:DM], w8[:, 0:1], None, ALU.mult))
            for k in range(1, 8):
                p.V(lambda e, k=k, par=par: e.scalar_tensor_tensor(
                    yt[par][:], ga[:, k * DM:(k + 1) * DM], w8[:, k:k + 1], yt[par][:], ALU.mult, ALU.add))
            yv = eV.n
            p.wait("sync", eV, yv)
            sts = str_[i]
            p.dma("sync", y_d[rows, :], yt[par][:], sts)
            store_done.append((sts, sts.n))
        for s in str_.sems:
            p.wait("sync", s, s.n)
        p.finish()
    return nc


def build_kv():
    nc = bass.Bass("TRN2", target_bir_lowering=False)
    xnT_d = nc.dram_tensor("xn1T", [DM, TPC], F32, kind="ExternalInput").ap()
    kw_d = nc.dram_tensor("k_wT", [DM, 128], F32, kind="ExternalInput").ap()
    vw_d = nc.dram_tensor("v_wT", [DM, 128], F32, kind="ExternalInput").ap()
    kT_d = nc.dram_tensor("kT", [128, TPC], BF16, kind="ExternalOutput").ap()
    vt_d = nc.dram_tensor("v_tok", [TPC, 128], BF16, kind="ExternalOutput").ap()
    ctx = ExitStack()
    with ctx:
        p = Prog(nc, ctx)
        eT, eV = p.e["tensor"], p.e["vector"]
        xnb = p.sbuf("xnb", [128, NDC * TPC], BF16)
        kwb = p.sbuf("kwb", [128, NDC * 128], BF16)
        vwb = p.sbuf("vwb", [128, NDC * 128], BF16)
        ko = p.sbuf("ko", [128, TPC], BF16)
        vo = p.sbuf("vo", [128, (TPC // 128) * 128], BF16)
        ps = [p.psum(f"ps{i}", [128, 512]) for i in range(2)]
        ld = p.sem("ld")
        st = p.sem("st")
        p.dma("gpsimd", xnb[:].rearrange("p (c t) -> p c t", c=NDC), fm_dram_ap(xnT_d, NDC), ld)
        p.dma("gpsimd", kwb[:].rearrange("p (c t) -> p c t", c=NDC), fm_dram_ap(kw_d, NDC), ld)
        p.dma("gpsimd", vwb[:].rearrange("p (c t) -> p c t", c=NDC), fm_dram_ap(vw_d, NDC), ld)
        p.wait("tensor", ld, ld.n)
        for th in range(2):
            mmn = None
            for c in range(NDC):
                mmn = p.T(lambda e, th=th, c=c: e.matmul(
                    ps[th % 2][:, :], kwb[:, c * 128:(c + 1) * 128],
                    xnb[:, c * TPC + th * 512: c * TPC + th * 512 + 512],
                    start=(c == 0), stop=(c == NDC - 1)))
            p.wait("vector", eT, mmn)
            ev = p.V(lambda e, th=th: e.tensor_copy(ko[:, th * 512:(th + 1) * 512], ps[th % 2][:, :]))
            p.wait("tensor", eV, ev)
        for i in range(TPC // 128):
            mmn = None
            for c in range(NDC):
                mmn = p.T(lambda e, i=i, c=c: e.matmul(
                    ps[i % 2][:, 0:128], xnb[:, c * TPC + i * 128: c * TPC + (i + 1) * 128],
                    vwb[:, c * 128:(c + 1) * 128], start=(c == 0), stop=(c == NDC - 1)))
            p.wait("vector", eT, mmn)
            ev = p.V(lambda e, i=i: e.tensor_copy(vo[:, i * 128:(i + 1) * 128], ps[i % 2][:, 0:128]))
            p.wait("tensor", eV, ev)
        p.wait("sync", eV, eV.n)
        p.dma("sync", kT_d[:], ko[:], st)
        p.dma("sync", vt_d.rearrange("(m p) d -> p m d", p=128), vo[:].rearrange("p (m d) -> p m d", d=128), st)
        p.wait("sync", st, st.n)
        p.finish()
    return nc


def build_attn():
    SCALE = float(1.0 / np.sqrt(128.0))
    NH = 16
    NMT = SEQ // 128
    nc = bass.Bass("TRN2", target_bir_lowering=False)
    xnT_d = nc.dram_tensor("xn1T", [DM, TPC], F32, kind="ExternalInput").ap()
    k_d = nc.dram_tensor("K", [128, SEQ], BF16, kind="ExternalInput").ap()
    v_d = nc.dram_tensor("V_tok", [SEQ, 128], BF16, kind="ExternalInput").ap()
    qw_d = nc.dram_tensor("qw_r", [NH, NDC, 128, 128], F32, kind="ExternalInput").ap()
    ow_d = nc.dram_tensor("ow_r", [NDC, NH, 128, 128], F32, kind="ExternalInput").ap()
    at_d = nc.dram_tensor("attnT", [DM, TPC], F32, kind="ExternalOutput").ap()
    ctx = ExitStack()
    with ctx:
        p = Prog(nc, ctx)
        eT, eV, eS = p.e["tensor"], p.e["vector"], p.e["scalar"]
        xnb = p.sbuf("xnb", [128, NDC * TPC], BF16)
        ksb = p.sbuf("ksb", [128, SEQ], BF16)
        vsb = p.sbuf("vsb", [128, NMT * 128], BF16)
        qsl = [p.sbuf(f"qsl{i}", [128, NDC * 128], BF16) for i in range(2)]
        osl = [p.sbuf(f"osl{i}", [128, NH * 128], BF16) for i in range(2)]
        qbf = p.sbuf("qbf", [128, NH * TPC], BF16)
        aob = p.sbuf("aob", [128, NH * TPC], BF16)
        q2 = p.sbuf("q2", [128, TPC])
        k2 = p.sbuf("k2", [128, SEQ])
        kn2 = p.sbuf("kn2", [1, SEQ])
        kmax2 = p.sbuf("kmax2", [1, 1])
        qn2 = p.sbuf("qn2", [1, TPC])
        mhat = p.sbuf("mhat", [1, TPC])
        rrow = p.sbuf("rrow", [1, 512])
        rbc = p.sbuf("rbc", [128, 512])
        scl = p.sbuf("scl", [128, 1])
        ones1f = p.sbuf("ones1f", [1, 128])
        onescf = p.sbuf("onescf", [128, 1])
        onesbf = p.sbuf("onesbf", [128, 1], BF16)
        prs = [p.sbuf(f"prs{i}", [128, 512], BF16) for i in range(2)]
        ao_st = [p.sbuf(f"ao_st{i}", [128, 512]) for i in range(2)]
        pss = [p.psum(f"pss{i}", [128, 512]) for i in range(2)]
        pssum = p.psum("pssum", [1, 512])
        psav = p.psum("psav", [128, 512])
        psbc = p.psum("psbc", [128, 512])
        ld = p.sem("ld")
        ldc = p.sem("ldc")
        ldq = Ring(p, "ldq", 2)
        str_ = Ring(p, "st", 2)

        p.V(lambda e: e.memset(ones1f[:], 1.0))
        p.V(lambda e: e.memset(onescf[:], 1.0))
        p.V(lambda e: e.memset(onesbf[:], 1.0))
        p.V(lambda e: e.memset(scl[:], SCALE))
        consts_v = p.e["vector"].n
        p.dma("gpsimd", xnb[:].rearrange("p (c t) -> p c t", c=NDC), fm_dram_ap(xnT_d, NDC), ldc)
        p.dma("sync", ksb[:], k_d[:], ld)
        p.dma("sync", vsb[:].rearrange("p (m d) -> p m d", d=128), v_d.rearrange("(m p) d -> p m d", p=128), ld)
        ldn0 = ld.n
        p.wait("tensor", ld, ldn0)
        p.wait("tensor", ldc, ldc.n)
        p.wait("scalar", ld, ldn0)
        p.wait("scalar", ldc, ldc.n)
        p.wait("vector", ld, ldn0)

        for oc in range(NH):
            par = oc % 2
            lds = ldq[oc]
            if oc >= 2:
                p.wait("gpsimd", eT, qmm_done[oc - 2])
            ldn = p.dma("gpsimd", qsl[par][:].rearrange("p (c i) -> p c i", c=NDC),
                        qw_d[oc].rearrange("c p i -> p c i"), lds)
            p.wait("tensor", lds, ldn)
            for th in range(2):
                mmn = None
                for c in range(NDC):
                    mmn = p.T(lambda e, par=par, c=c, th=th: e.matmul(
                        pss[th % 2][:, :], qsl[par][:, c * 128:(c + 1) * 128],
                        xnb[:, c * TPC + th * 512: c * TPC + th * 512 + 512],
                        start=(c == 0), stop=(c == NDC - 1)))
                if oc == 0 and th == 0:
                    qmm_done = {}
                qmm_done[oc] = mmn
                p.wait("vector", eT, mmn)
                ev = p.V(lambda e, oc=oc, th=th: e.tensor_scalar(
                    qbf[:, oc * TPC + th * 512: oc * TPC + th * 512 + 512],
                    pss[th % 2][:, :], scl[:], None, ALU.mult))
                p.wait("tensor", eV, ev)
        a1 = p.S(lambda e: e.activation(k2[:], ksb[:], AF.Square))
        p.wait("tensor", eS, a1)
        for j in range(8):
            mmn = p.T(lambda e, j=j: e.matmul(
                pssum[:, :], onescf[:], k2[:, j * 512:(j + 1) * 512], start=True, stop=True))
            p.wait("vector", eT, mmn)
            ev = p.V(lambda e, j=j: e.tensor_copy(kn2[:, j * 512:(j + 1) * 512], pssum[:, :]))
            p.wait("tensor", eV, ev)
        p.V(lambda e: e.reduce_max(kmax2[:], kn2[:], axis=AX.X))
        km_v = p.e["vector"].n
        st_i = 0
        store_done = []
        q_ready = p.e["vector"].n
        p.wait("scalar", eV, q_ready)
        for h in range(NH):
            a2 = p.S(lambda e, h=h: e.activation(q2[:], qbf[:, h * TPC:(h + 1) * TPC], AF.Square))
            p.wait("tensor", eS, a2)
            for th in range(2):
                mmn = p.T(lambda e, h=h, th=th: e.matmul(
                    pssum[:, :], onescf[:], q2[:, th * 512:(th + 1) * 512], start=True, stop=True))
                p.wait("vector", eT, mmn)
                ev = p.V(lambda e, th=th: e.tensor_copy(qn2[:, th * 512:(th + 1) * 512], pssum[:, :]))
                p.wait("tensor", eV, ev)
            p.wait("scalar", eV, km_v)
            p.wait("scalar", eV, p.e["vector"].n)
            amh = p.S(lambda e: e.activation(mhat[:], qn2[:], AF.Sqrt, scale=kmax2[:]))
            p.wait("vector", eS, amh)
            nmh = p.V(lambda e: e.tensor_single_scalar(mhat[:], mhat[:], -1.0, ALU.mult))
            p.wait("tensor", eV, nmh)
            for th in range(2):
                tq = slice(th * 512, (th + 1) * 512)
                for mt in range(NMT):
                    par = mt % 2
                    p.T(lambda e, h=h, mt=mt, tq=tq, par=par: e.matmul(
                        pss[par][:, :], ksb[:, mt * 128:(mt + 1) * 128],
                        qbf[:, h * TPC + tq.start: h * TPC + tq.stop], start=True, stop=False))
                    mmb = p.T(lambda e, tq=tq, par=par: e.matmul(
                        pss[par][:, :], ones1f[:, 0:128], mhat[:, tq], start=False, stop=True))
                    p.wait("scalar", eT, mmb)
                    aex = p.S(lambda e, par=par: e.activation(prs[par][:], pss[par][:, :], AF.Exp))
                    p.wait("tensor", eS, aex)
                    p.T(lambda e, mt=mt, par=par: e.matmul(
                        pssum[:, :], onesbf[:], prs[par][:], start=(mt == 0), stop=(mt == NMT - 1)))
                    p.T(lambda e, mt=mt, par=par: e.matmul(
                        psav[:, :], vsb[:, mt * 128:(mt + 1) * 128], prs[par][:],
                        start=(mt == 0), stop=(mt == NMT - 1)))
                avm = p.e["tensor"].n
                p.wait("vector", eT, avm)
                p.V(lambda e: e.reciprocal(rrow[:], pssum[:, :]))
                rv = p.e["vector"].n
                p.wait("tensor", eV, rv)
                bcm = p.T(lambda e: e.matmul(psbc[:, :], ones1f[:, 0:128], rrow[:], start=True, stop=True))
                p.wait("vector", eT, bcm)
                p.V(lambda e: e.tensor_copy(rbc[:], psbc[:, :]))
                ev = p.V(lambda e, h=h, tq=tq: e.tensor_tensor(
                    aob[:, h * TPC + tq.start: h * TPC + tq.stop], psav[:, :], rbc[:], ALU.mult))
                p.wait("tensor", eV, ev)
        ldo = Ring(p, "ldo", 2)
        omm_done = {}
        for dc in range(NDC):
            opar = dc % 2
            if dc >= 2:
                p.wait("gpsimd", eT, omm_done[dc - 2])
            ldn = p.dma("gpsimd", osl[opar][:].rearrange("p (c i) -> p c i", c=NH),
                        ow_d[dc].rearrange("c p i -> p c i"), ldo[dc])
            p.wait("tensor", ldo[dc], ldn)
            for th in range(2):
                mmn = None
                for oc in range(NH):
                    mmn = p.T(lambda e, dc=dc, th=th, oc=oc, opar=opar: e.matmul(
                        pss[(dc * 2 + th) % 2][:, :], osl[opar][:, oc * 128:(oc + 1) * 128],
                        aob[:, oc * TPC + th * 512: oc * TPC + th * 512 + 512],
                        start=(oc == 0), stop=(oc == NH - 1)))
                omm_done[dc] = mmn
                p.wait("vector", eT, mmn)
                if st_i >= 2:
                    p.wait("vector", *store_done[st_i - 2])
                ev = p.V(lambda e, dc=dc, th=th: e.tensor_copy(ao_st[(dc * 2 + th) % 2][:], pss[(dc * 2 + th) % 2][:, :]))
                p.wait("tensor", eV, ev)
                p.wait("sync", eV, ev)
                sts = str_[st_i]
                p.dma("sync", at_d[dc * 128:(dc + 1) * 128, th * 512:(th + 1) * 512],
                      ao_st[(dc * 2 + th) % 2][:], sts)
                store_done.append((sts, sts.n))
                st_i += 1
        for s in str_.sems:
            p.wait("sync", s, s.n)
        p.finish()
    return nc


def build_moe_final(debug=False):
    BIG = 65536.0
    nc = bass.Bass("TRN2", target_bir_lowering=False)
    xnT_d = nc.dram_tensor("xn1T", [DM, TPC], F32, kind="ExternalInput").ap()
    rw_d = nc.dram_tensor("router_wT", [DM, 16], F32, kind="ExternalInput").ap()
    eb_d = nc.dram_tensor("e_bias", [1, 16], F32, kind="ExternalInput").ap()
    wup_d = nc.dram_tensor("w_up_r", [16, 8, NDC, 128, 128], F32, kind="ExternalInput").ap()
    wdn_d = nc.dram_tensor("w_down_r", [16, NDC, 8, 128, 128], F32, kind="ExternalInput").ap()
    shu_d = nc.dram_tensor("shup_r", [32, NDC, 128, 128], F32, kind="ExternalInput").ap()
    shd_d = nc.dram_tensor("shdown_r", [NDC, 32, 128, 128], F32, kind="ExternalInput").ap()
    x1_d = nc.dram_tensor("x1T", [DM, TPC], F32, kind="ExternalInput").ap()
    at_d = nc.dram_tensor("attnT", [DM, TPC], F32, kind="ExternalInput").ap()
    out_d = nc.dram_tensor("outT", [DM, TPC], F32, kind="ExternalOutput").ap()
    if debug:
        dbgl_d = nc.dram_tensor("dbg_logits", [16, TPC], F32, kind="ExternalOutput").ap()
        dbgc_d = nc.dram_tensor("dbg_comb", [16, TPC], F32, kind="ExternalOutput").ap()
        dbga_d = nc.dram_tensor("dbg_acc", [DM, TPC], F32, kind="ExternalOutput").ap()
    ntiles = TPC // 128
    ctx = ExitStack()
    with ctx:
        p = Prog(nc, ctx)
        eT, eV, eS, eG = p.e["tensor"], p.e["vector"], p.e["scalar"], p.e["gpsimd"]
        xnb = p.sbuf("xnb", [128, NDC * TPC], BF16)
        acc = p.sbuf("acc", [128, NDC * TPC])
        act_e = p.sbuf("act_e", [128, 8 * TPC], BF16)
        upsl = p.sbuf("upsl", [128, 4 * NDC * 128], BF16)
        dnsl = p.sbuf("dnsl", [128, 8 * 8 * 128], BF16)
        rw = p.sbuf("rw", [128, NDC * 16])
        logitsT = p.sbuf("logitsT", [16, TPC])
        xs = [p.sbuf(f"xs{i}", [128, 512]) for i in range(2)]
        ident = p.sbuf("ident", [128, 128])
        iota16 = p.sbuf("iota16", [128, 16])
        esel = p.sbuf("esel16", [16, DM])
        tsel = p.sbuf("tsel16", [16, DM])
        ebr = p.sbuf("ebr", [1, 16])
        ebias_bc = p.sbuf("ebias_bc", [128, 16])
        combT = p.sbuf("combT", [16, TPC])
        combbc = p.sbuf("combbc", [128, TPC])
        sct = p.sbuf("sct", [128, 16])
        sbt = p.sbuf("sbt", [128, 16])
        ptmp = p.sbuf("ptmp", [128, 6 * 4])
        gs = p.sbuf("gs", [128, 4])
        gs8 = p.sbuf("gs8", [128, 8])
        gsort = p.sbuf("gsort", [128, 8])
        gmask = p.sbuf("gmask", [128, 4])
        mskd = p.sbuf("mskd", [128, 16])
        msk2 = p.sbuf("msk2", [128, 16])
        eqt = p.sbuf("eqt", [128, 16])
        eqi1 = p.sbuf("eqi1", [128, 16])
        eqi2 = p.sbuf("eqi2", [128, 16])
        tt = p.sbuf("tt", [128, 16])
        junk16 = p.sbuf("junk16", [128, 16])
        m1 = p.sbuf("m1", [128, 1])
        i1 = p.sbuf("i1", [128, 1])
        i2 = p.sbuf("i2", [128, 1])
        w1c = p.sbuf("w1c", [128, 1])
        w2c = p.sbuf("w2c", [128, 1])
        den = p.sbuf("den", [128, 1])
        rden = p.sbuf("rden", [128, 1])
        w1n = p.sbuf("w1n", [128, 1])
        w2n = p.sbuf("w2n", [128, 1])
        comb_t = p.sbuf("comb_t", [128, 16])
        r_sb = p.sbuf("r_sb", [128, 512])
        sq_sb = p.sbuf("sq_sb", [128, 512])
        x1c = [p.sbuf(f"x1c{i}", [128, 512]) for i in range(2)]
        atc = [p.sbuf(f"atc{i}", [128, 512]) for i in range(2)]
        ost = [p.sbuf(f"ost{i}", [128, 512]) for i in range(2)]
        psr = p.psum("psr", [16, 512])
        pst = p.psum("pst", [128, 512])
        psh = [p.psum(f"psh{i}", [128, 512]) for i in range(2)]
        psd = [p.psum(f"psd{i}", [128, 512]) for i in range(2)]
        ldc = p.sem("ldc")
        ld0 = p.sem("ld0")
        ldx = Ring(p, "ldx", 2)
        ldw = Ring(p, "ldw", 2)
        ldf = Ring(p, "ldf", 2)
        str_ = Ring(p, "st", 2)

        p.G(lambda e: e.memset(ident[:], 0.0))
        p.G(lambda e: e.affine_select(out=ident[:], in_=ident[:], compare_op=ALU.not_equal,
                                      fill=1.0, base=0, pattern=[[-1, 128]], channel_multiplier=1))
        p.G(lambda e: e.iota(iota16[:], pattern=[[1, 16]], base=0, channel_multiplier=0,
                             allow_small_or_imprecise_dtypes=True))
        p.G(lambda e: e.iota(esel[:].rearrange("k (n q) -> k n q", q=128),
                             pattern=[[1, 16], [0, 128]], base=0, channel_multiplier=0,
                             allow_small_or_imprecise_dtypes=True))
        gsel = p.G(lambda e: e.iota(tsel[:], pattern=[[0, DM]], base=0, channel_multiplier=1,
                                    allow_small_or_imprecise_dtypes=True))
        p.wait("vector", eG, gsel)
        esel_v = p.V(lambda e: e.tensor_tensor(esel[:], esel[:], tsel[:], ALU.is_equal))
        p.V(lambda e: e.memset(acc[:], 0.0))
        p.V(lambda e: e.memset(gs8[:], -BIG))
        p.dma("gpsimd", xnb[:].rearrange("p (c t) -> p c t", c=NDC), fm_dram_ap(xnT_d, NDC), ldc)
        p.dma("sync", rw[:].rearrange("p (c t) -> p c t", c=NDC), fm_dram_ap(rw_d, NDC), ld0)
        p.dma("sync", ebr[:], eb_d[:], ld0)
        ld0n = ld0.n
        p.wait("tensor", ld0, ld0n)
        p.wait("tensor", eV, esel_v)
        ones1f = p.sbuf("ones1f", [1, 128])
        ov = p.V(lambda e: e.memset(ones1f[:], 1.0))
        p.wait("tensor", eV, ov)
        bm = p.T(lambda e: e.matmul(pst[:, 0:16], ones1f[:], ebr[:], start=True, stop=True))
        p.wait("vector", eT, bm)
        ebv = p.V(lambda e: e.tensor_copy(ebias_bc[:], pst[:, 0:16]))
        p.wait("tensor", eV, ebv)

        for th in range(2):
            mmn = None
            for c in range(NDC):
                lds = ldx[th * NDC + c]
                ldn = p.dma("sync", xs[c % 2][:],
                            xnT_d[c * 128:(c + 1) * 128, th * 512:(th + 1) * 512], lds)
                p.wait("tensor", lds, ldn)
                mmn = p.T(lambda e, c=c: e.matmul(
                    psr[:, :], rw[:, c * 16:(c + 1) * 16], xs[c % 2][:],
                    start=(c == 0), stop=(c == NDC - 1)))
                if c < NDC - 1:
                    p.wait("sync", eT, mmn)
            p.wait("vector", eT, mmn)
            lv = p.V(lambda e, th=th: e.tensor_copy(logitsT[:, th * 512:(th + 1) * 512], psr[:, :]))
            p.wait("tensor", eV, lv)
        log_v = p.e["vector"].n
        p.wait("tensor", eV, log_v)

        gv = [(0, 1), (0, 2), (0, 3), (1, 2), (1, 3), (2, 3)]
        for i in range(ntiles):
            p.wait("tensor", eV, p.e["vector"].n)
            tr = p.T(lambda e, i=i: e.transpose(pst[:, 0:16], logitsT[:, i * 128:(i + 1) * 128], ident[0:16, 0:16]))
            p.wait("scalar", eT, tr)
            sg = p.S(lambda e: e.activation(sct[:], pst[:, 0:16], AF.Sigmoid))
            p.wait("vector", eS, sg)
            p.wait("tensor", eS, sg)
            p.V(lambda e: e.tensor_tensor(sbt[:], sct[:], ebias_bc[:], ALU.add))
            gvv = sbt[:].rearrange("p (g j) -> p g j", j=4)
            for pi, (a, b) in enumerate(gv):
                p.V(lambda e, a=a, b=b, pi=pi: e.tensor_tensor(
                    ptmp[:, pi * 4:(pi + 1) * 4],
                    sbt[:].rearrange("p (g j) -> p g j", j=4)[:, :, a],
                    sbt[:].rearrange("p (g j) -> p g j", j=4)[:, :, b], ALU.add))
            p.V(lambda e: e.tensor_tensor(gs[:], ptmp[:, 0:4], ptmp[:, 4:8], ALU.max))
            for pi in range(2, 6):
                p.V(lambda e, pi=pi: e.tensor_tensor(gs[:], gs[:], ptmp[:, pi * 4:(pi + 1) * 4], ALU.max))
            p.V(lambda e: e.tensor_copy(gs8[:, 0:4], gs[:]))
            p.V(lambda e: e.max(out=gsort[:], in_=gs8[:]))
            p.V(lambda e: e.tensor_scalar(gmask[:], gs[:], gsort[:, 1:2], None, ALU.is_ge))
            p.V(lambda e: e.tensor_tensor(
                mskd[:].rearrange("p (g j) -> p g j", j=4),
                sbt[:].rearrange("p (g j) -> p g j", j=4),
                gmask[:].rearrange("p (g o) -> p g o", o=1).to_broadcast([128, 4, 4]),
                ALU.mult))
            p.V(lambda e: e.reduce_max(m1[:], mskd[:], axis=AX.X))
            p.V(lambda e: e.tensor_scalar(eqt[:], mskd[:], m1[:], None, ALU.is_equal))
            p.V(lambda e: e.tensor_tensor(tt[:], eqt[:], iota16[:], ALU.mult))
            p.V(lambda e: e.scalar_tensor_tensor(tt[:], eqt[:], -BIG, tt[:], ALU.mult, ALU.add))
            p.V(lambda e: e.tensor_single_scalar(tt[:], tt[:], BIG, ALU.add))
            p.V(lambda e: e.tensor_reduce(i1[:], tt[:], axis=AX.X, op=ALU.min))
            p.V(lambda e: e.tensor_scalar(eqi1[:], iota16[:], i1[:], None, ALU.is_equal))
            p.V(lambda e: e.scalar_tensor_tensor(junk16[:], eqi1[:], 0.0, sct[:], ALU.bypass, ALU.mult,
                                                 accum_out=w1c[:]))
            p.V(lambda e: e.scalar_tensor_tensor(msk2[:], eqi1[:], -BIG, mskd[:], ALU.mult, ALU.add))
            p.V(lambda e: e.reduce_max(m1[:], msk2[:], axis=AX.X))
            p.V(lambda e: e.tensor_scalar(eqt[:], msk2[:], m1[:], None, ALU.is_equal))
            p.V(lambda e: e.tensor_tensor(tt[:], eqt[:], iota16[:], ALU.mult))
            p.V(lambda e: e.scalar_tensor_tensor(tt[:], eqt[:], -BIG, tt[:], ALU.mult, ALU.add))
            p.V(lambda e: e.tensor_single_scalar(tt[:], tt[:], BIG, ALU.add))
            p.V(lambda e: e.tensor_reduce(i2[:], tt[:], axis=AX.X, op=ALU.min))
            p.V(lambda e: e.tensor_scalar(eqi2[:], iota16[:], i2[:], None, ALU.is_equal))
            p.V(lambda e: e.scalar_tensor_tensor(junk16[:], eqi2[:], 0.0, sct[:], ALU.bypass, ALU.mult,
                                                 accum_out=w2c[:]))
            p.V(lambda e: e.tensor_tensor(den[:], w1c[:], w2c[:], ALU.add))
            p.V(lambda e: e.tensor_single_scalar(den[:], den[:], 1e-20, ALU.add))
            p.V(lambda e: e.reciprocal(rden[:], den[:]))
            p.V(lambda e: e.tensor_tensor(w1n[:], w1c[:], rden[:], ALU.mult))
            p.V(lambda e: e.tensor_tensor(w2n[:], w2c[:], rden[:], ALU.mult))
            p.V(lambda e: e.tensor_scalar(comb_t[:], eqi1[:], w1n[:], None, ALU.mult))
            cv = p.V(lambda e: e.scalar_tensor_tensor(comb_t[:], eqi2[:], w2n[:], comb_t[:], ALU.mult, ALU.add))
            p.wait("tensor", eV, cv)
            tr2 = p.T(lambda e, i=i: e.transpose(pst[0:16, 0:128], comb_t[:], ident[:]))
            p.wait("vector", eT, tr2)
            p.V(lambda e, i=i: e.tensor_copy(combT[:, i * 128:(i + 1) * 128], pst[0:16, 0:128]))
        comb_v = p.e["vector"].n
        p.wait("tensor", eV, comb_v)
        p.wait("tensor", ldc, ldc.n)

        if debug:
            dbg_st = p.sem("dbg_st")
            p.wait("sync", eV, comb_v)
            p.dma("sync", dbgl_d[:], logitsT[:], dbg_st)
            p.dma("sync", dbgc_d[:], combT[:], dbg_st)
            p.wait("sync", dbg_st, dbg_st.n)
        wl = 0
        upsl_free = None
        dnsl_free = None
        for ex in range(16):
            for th in range(2):
                bmn = p.T(lambda e, ex=ex, th=th: e.matmul(
                    pst[:, 0:512], esel[:, ex * 128:(ex + 1) * 128],
                    combT[:, th * 512:(th + 1) * 512], start=True, stop=True))
                p.wait("vector", eT, bmn)
                cbv = p.V(lambda e, th=th: e.tensor_copy(combbc[:, th * 512:(th + 1) * 512], pst[:, 0:512]))
                p.wait("tensor", eV, cbv)
            for ih in range(2):
                lds = ldw[wl]
                if upsl_free is not None:
                    p.wait("gpsimd", eV, upsl_free)
                ldn = p.dma("gpsimd", upsl[:].rearrange("p (a c i) -> p a c i", a=4, c=NDC),
                            wup_d[ex, ih * 4:(ih + 1) * 4].rearrange("a c p i -> p a c i"), lds)
                wl += 1
                p.wait("tensor", lds, ldn)
                for ic4 in range(4):
                    ic = ih * 4 + ic4
                    for th in range(2):
                        par = (ic * 2 + th) % 2
                        mmn = None
                        for dc in range(NDC):
                            mmn = p.T(lambda e, ic4=ic4, dc=dc, th=th, par=par: e.matmul(
                                psh[par][:, :], upsl[:, (ic4 * NDC + dc) * 128:(ic4 * NDC + dc + 1) * 128],
                                xnb[:, dc * TPC + th * 512: dc * TPC + th * 512 + 512],
                                start=(dc == 0), stop=(dc == NDC - 1)))
                        p.wait("scalar", eT, mmn)
                        rl = p.S(lambda e, par=par: e.activation(r_sb[:], psh[par][:, :], AF.Relu))
                        sq = p.S(lambda e: e.activation(sq_sb[:], r_sb[:], AF.Square))
                        p.wait("vector", eS, sq)
                        av = p.V(lambda e, ic=ic, th=th: e.tensor_tensor(
                            act_e[:, ic * TPC + th * 512: ic * TPC + th * 512 + 512],
                            sq_sb[:], combbc[:, th * 512:(th + 1) * 512], ALU.mult))
                        p.wait("tensor", eV, av)
                upsl_free = p.e["vector"].n
            for dh in range(2):
                lds = ldw[wl]
                if dnsl_free is not None:
                    p.wait("gpsimd", eT, dnsl_free)
                ldn = p.dma("gpsimd", dnsl[:].rearrange("p (a c i) -> p a c i", a=8, c=8),
                            wdn_d[ex, dh * 8:(dh + 1) * 8].rearrange("a c p i -> p a c i"), lds)
                wl += 1
                p.wait("tensor", lds, ldn)
                for dc8 in range(8):
                    dc = dh * 8 + dc8
                    for th in range(2):
                        par = (dc * 2 + th) % 2
                        mmn = None
                        for ic in range(8):
                            mmn = p.T(lambda e, dc8=dc8, ic=ic, th=th, par=par: e.matmul(
                                psd[par][:, :], dnsl[:, (dc8 * 8 + ic) * 128:(dc8 * 8 + ic + 1) * 128],
                                act_e[:, ic * TPC + th * 512: ic * TPC + th * 512 + 512],
                                start=(ic == 0), stop=(ic == 7)))
                        p.wait("vector", eT, mmn)
                        avv = p.V(lambda e, dc=dc, th=th, par=par: e.tensor_tensor(
                            acc[:, dc * TPC + th * 512: dc * TPC + th * 512 + 512],
                            psd[par][:, :],
                            acc[:, dc * TPC + th * 512: dc * TPC + th * 512 + 512], ALU.add))
                        p.wait("tensor", eV, avv)
                dnsl_free = p.e["tensor"].n

        for tq in range(4):
            qsl_ = slice(tq * 256, (tq + 1) * 256)
            for ih in range(8):
                lds = ldw[wl]
                if upsl_free is not None:
                    p.wait("gpsimd", eV, upsl_free)
                ldn = p.dma("gpsimd", upsl[:].rearrange("p (a c i) -> p a c i", a=4, c=NDC),
                            shu_d[ih * 4:(ih + 1) * 4].rearrange("a c p i -> p a c i"), lds)
                wl += 1
                p.wait("tensor", lds, ldn)
                for ic4 in range(4):
                    ic = ih * 4 + ic4
                    par = ic % 2
                    mmn = None
                    for dc in range(NDC):
                        mmn = p.T(lambda e, ic4=ic4, dc=dc, par=par, qsl_=qsl_: e.matmul(
                            psh[par][:, 0:256], upsl[:, (ic4 * NDC + dc) * 128:(ic4 * NDC + dc + 1) * 128],
                            xnb[:, dc * TPC + qsl_.start: dc * TPC + qsl_.stop],
                            start=(dc == 0), stop=(dc == NDC - 1)))
                    p.wait("scalar", eT, mmn)
                    rl = p.S(lambda e, par=par: e.activation(r_sb[:, 0:256], psh[par][:, 0:256], AF.Relu))
                    sq = p.S(lambda e: e.activation(sq_sb[:, 0:256], r_sb[:, 0:256], AF.Square))
                    p.wait("vector", eS, sq)
                    av = p.V(lambda e, ic=ic: e.tensor_copy(
                        act_e[:, ic * 256:(ic + 1) * 256], sq_sb[:, 0:256]))
                    p.wait("tensor", eV, av)
                upsl_free = p.e["vector"].n
            for dc2 in range(8):
                lds = ldw[wl]
                if dnsl_free is not None:
                    p.wait("gpsimd", eT, dnsl_free)
                ldn = p.dma("gpsimd", dnsl[:].rearrange("p (a c i) -> p a c i", a=2, c=32),
                            shd_d[dc2 * 2:(dc2 + 1) * 2].rearrange("a c p i -> p a c i"), lds)
                wl += 1
                p.wait("tensor", lds, ldn)
                for j in range(2):
                    dc = dc2 * 2 + j
                    par = dc % 2
                    mmn = None
                    for ic in range(32):
                        mmn = p.T(lambda e, j=j, ic=ic, par=par: e.matmul(
                            psd[par][:, 0:256], dnsl[:, (j * 32 + ic) * 128:(j * 32 + ic + 1) * 128],
                            act_e[:, ic * 256:(ic + 1) * 256],
                            start=(ic == 0), stop=(ic == 31)))
                    p.wait("vector", eT, mmn)
                    avv = p.V(lambda e, dc=dc, tq=tq, par=par: e.tensor_tensor(
                        acc[:, dc * TPC + tq * 256: dc * TPC + (tq + 1) * 256],
                        psd[par][:, 0:256],
                        acc[:, dc * TPC + tq * 256: dc * TPC + (tq + 1) * 256], ALU.add))
                    p.wait("tensor", eV, avv)
                dnsl_free = p.e["tensor"].n

        if debug:
            dbg_st2 = p.sem("dbg_st2")
            p.wait("sync", eV, p.e["vector"].n)
            p.dma("sync", fm_dram_ap(dbga_d, NDC), acc[:].rearrange("p (c t) -> p c t", c=NDC), dbg_st2)
            p.wait("sync", dbg_st2, dbg_st2.n)
        store_done = []
        fi = 0
        for dc in range(NDC):
            for th in range(2):
                par = fi % 2
                lds = ldf[fi]
                csl = slice(dc * TPC + th * 512, dc * TPC + th * 512 + 512)
                dsl_ = (slice(dc * 128, (dc + 1) * 128), slice(th * 512, (th + 1) * 512))
                if fi >= 2:
                    p.wait("sync", *store_done[fi - 2])
                p.dma("sync", x1c[par][:], x1_d[dsl_[0], dsl_[1]], lds)
                ldn = p.dma("sync", atc[par][:], at_d[dsl_[0], dsl_[1]], lds)
                p.wait("vector", lds, ldn)
                p.V(lambda e, par=par, csl=csl: e.tensor_tensor(ost[par][:], acc[:, csl], x1c[par][:], ALU.add))
                fv = p.V(lambda e, par=par: e.tensor_tensor(ost[par][:], ost[par][:], atc[par][:], ALU.add))
                p.wait("sync", eV, fv)
                sts = str_[fi]
                p.dma("sync", out_d[dsl_[0], dsl_[1]], ost[par][:], sts)
                store_done.append((sts, sts.n))
                fi += 1
        for s in str_.sems:
            p.wait("sync", s, s.n)
        p.finish()
    return nc


def _runs(nc, maps):
    return run_bass_kernel_spmd(nc, maps, core_ids=list(range(NCORES))).results


def kernel_unfused(**inputs):
    C = np.ascontiguousarray
    f32 = np.float32
    x = C(np.asarray(inputs["L0_attn_x"], f32).reshape(NCORES * TPC, DM))
    TT = NCORES * TPC

    def shards_rows(a):
        return [C(a[c * TPC:(c + 1) * TPC]) for c in range(NCORES)]

    def shards_cols(a):
        return [C(a[:, c * TPC:(c + 1) * TPC]) for c in range(NCORES)]

    nc = build_ln(0, False)
    res = _runs(nc, [dict(x=s) for s in shards_rows(x)])
    xn = np.concatenate([r["xn"] for r in res], axis=0)
    xnT = C(xn.T)

    W_inT = C(np.asarray(inputs["L0_attn_W_in"], f32).T)
    W_dtT = C(np.asarray(inputs["L0_attn_W_dt"], f32).T)
    b_dt_r = C(np.asarray(inputs["L0_attn_b_dt"], f32).reshape(16, 128).T)
    nc = build_dbc_delta()
    res = _runs(nc, [dict(xnT=s, W_inT=W_inT, W_dtT=W_dtT, b_dt_r=b_dt_r) for s in shards_cols(xnT)])
    dbcT = np.concatenate([r["dbcT"] for r in res], axis=1)
    deltaT = np.concatenate([r["deltaT"] for r in res], axis=1)

    W_q = C(np.asarray(inputs["L0_ffn_W_q"], f32))
    nc = build_qproj()
    res = _runs(nc, [dict(xnT=s, W_q=W_q) for s in shards_cols(xnT)])
    qfull = np.concatenate([r["qT"] for r in res], axis=1).T

    A_log = np.asarray(inputs["L0_attn_A_log"], f32)
    Dvec = np.asarray(inputs["L0_attn_D"], f32)
    Bt = C(dbcT[64:80])
    Ct = C(dbcT[80:96])
    nc = build_scan()
    maps = []
    for c in range(NCORES):
        sl = slice(c * 256, (c + 1) * 256)
        maps.append(dict(
            deltaT_sl=C(deltaT[sl]), xnT_sl=C(xnT[sl]), Bt=Bt, Ct=Ct,
            A_log_r=C(A_log[sl].reshape(2, 128, 16).transpose(1, 0, 2).reshape(128, 32)),
            D_r=C(Dvec[sl].reshape(2, 128).T)))
    res = _runs(nc, maps)
    y_mamba = np.concatenate([r["yT_sl"] for r in res], axis=0).T

    gate_r = C(np.asarray(inputs["L0_ffn_gate_w"], f32).reshape(16, 128, 64, 128).transpose(2, 0, 1, 3))
    up_r = C(np.asarray(inputs["L0_ffn_up_w"], f32).reshape(16, 128, 64, 128).transpose(2, 0, 1, 3))
    down_r = C(np.asarray(inputs["L0_ffn_down_w"], f32).reshape(64, 128, 16, 128).transpose(2, 0, 1, 3))
    nc = build_mlp()
    res = _runs(nc, [dict(xnT=s, gate_r=gate_r, up_r=up_r, down_r=down_r) for s in shards_cols(xnT)])
    mlp_tok = np.concatenate([r["mlpT"] for r in res], axis=1).T

    Q0 = C(qfull[:TT // 2].reshape(TT, 64))
    Q1 = C(qfull[TT // 2:].reshape(TT, 64))
    keys_r = C(np.asarray(inputs["L0_ffn_keys"], f32).transpose(1, 0, 2).reshape(64, 128))
    de_w = C(np.asarray(inputs["L0_ffn_down_embed"], f32))
    ue_w = C(np.asarray(inputs["L0_ffn_up_embed"], f32))
    nc = build_peer()
    maps = []
    for c in range(NCORES):
        sl = slice(c * TPC, (c + 1) * TPC)
        maps.append(dict(xn_tok=C(xn[sl]), Q0T=C(Q0[sl].T), Q1T=C(Q1[sl].T),
                         keys_r=keys_r, down_embed=de_w, up_embed=ue_w))
    res = _runs(nc, maps)
    peer_tok = np.concatenate([r["peer_tok"] for r in res], axis=0)

    nc = build_ln(3, True)
    maps = []
    for c in range(NCORES):
        sl = slice(c * TPC, (c + 1) * TPC)
        maps.append(dict(x=C(x[sl]), add0=C(y_mamba[sl]), add1=C(mlp_tok[sl]), add2=C(peer_tok[sl])))
    res = _runs(nc, maps)
    xn1 = np.concatenate([r["xn"] for r in res], axis=0)
    x1 = np.concatenate([r["xsum"] for r in res], axis=0)
    xn1T = C(xn1.T)
    x1T = C(x1.T)

    k_wT = C(np.asarray(inputs["L1_attn_k_w"], f32).T)
    v_wT = C(np.asarray(inputs["L1_attn_v_w"], f32).T)
    nc = build_kv()
    res = _runs(nc, [dict(xn1T=s, k_wT=k_wT, v_wT=v_wT) for s in shards_cols(xn1T)])
    kT = np.concatenate([r["kT"] for r in res], axis=1)
    v_tok = np.concatenate([r["v_tok"] for r in res], axis=0)

    qw_r = C(np.asarray(inputs["L1_attn_q_w"], f32).T.reshape(16, 128, 16, 128).transpose(2, 0, 1, 3))
    ow_r = C(np.asarray(inputs["L1_attn_o_w"], f32).T.reshape(16, 128, 16, 128).transpose(2, 0, 1, 3))
    nc = build_attn()
    maps = []
    for c in range(NCORES):
        b = c // 4
        Kb = C(kT[:, b * SEQ:(b + 1) * SEQ])
        Vb = C(v_tok[b * SEQ:(b + 1) * SEQ])
        maps.append(dict(xn1T=C(xn1T[:, c * TPC:(c + 1) * TPC]), K=Kb, V_tok=Vb,
                         qw_r=qw_r, ow_r=ow_r))
    res = _runs(nc, maps)
    attnT = np.concatenate([r["attnT"] for r in res], axis=1)

    rwT = C(np.asarray(inputs["L1_ffn_router_weight"], f32).T)
    ebias = C(np.asarray(inputs["L1_ffn_e_bias"], f32)[None, :])
    w_up_r = C(np.asarray(inputs["L1_ffn_w_up"], f32).reshape(16, 16, 128, 8, 128).transpose(0, 3, 1, 2, 4))
    w_down_r = C(np.asarray(inputs["L1_ffn_w_down"], f32).reshape(16, 8, 128, 16, 128).transpose(0, 3, 1, 2, 4))
    shup_r = C(np.asarray(inputs["L1_ffn_shared_up"], f32).reshape(16, 128, 32, 128).transpose(2, 0, 1, 3))
    shdown_r = C(np.asarray(inputs["L1_ffn_shared_down"], f32).reshape(32, 128, 16, 128).transpose(2, 0, 1, 3))
    nc = build_moe_final()
    maps = []
    for c in range(NCORES):
        sl = slice(c * TPC, (c + 1) * TPC)
        maps.append(dict(xn1T=C(xn1T[:, sl]), router_wT=rwT, e_bias=ebias,
                         w_up_r=w_up_r, w_down_r=w_down_r, shup_r=shup_r, shdown_r=shdown_r,
                         x1T=C(x1T[:, sl]), attnT=C(attnT[:, sl])))
    res = _runs(nc, maps)
    outT = np.concatenate([r["outT"] for r in res], axis=1)
    return C(outT.T.reshape(2, SEQ, DM)).astype(np.float32)


BIG = 65536.0
NH = 16
NT = TPC // 128


def build_fused():
    nc = bass.Bass("TRN2", target_bir_lowering=False)

    x_d = nc.dram_tensor("x_tok", [TPC, DM], F32, kind="ExternalInput").ap()
    winT_d = nc.dram_tensor("W_inT", [DM, 96], F32, kind="ExternalInput").ap()
    wdtT_d = nc.dram_tensor("W_dtT", [64, DM], F32, kind="ExternalInput").ap()
    bdt_d = nc.dram_tensor("b_dt_r", [128, NDC], F32, kind="ExternalInput").ap()
    alog_d = nc.dram_tensor("A_log_r", [128, 256], F32, kind="ExternalInput").ap()
    dr_d = nc.dram_tensor("D_r", [128, NDC], F32, kind="ExternalInput").ap()
    wq_d = nc.dram_tensor("W_q", [DM, 128], F32, kind="ExternalInput").ap()
    keys_d = nc.dram_tensor("keys_own", [64, 64], F32, kind="ExternalInput").ap()
    alpha_d = nc.dram_tensor("alpha", [128, 8], F32, kind="ExternalInput").ap()
    gate_d = nc.dram_tensor("gate_r", [64, NDC, 128, 128], F32, kind="ExternalInput").ap()
    up_d = nc.dram_tensor("up_r", [64, NDC, 128, 128], F32, kind="ExternalInput").ap()
    down_d = nc.dram_tensor("down_r", [NDC, 64, 128, 128], F32, kind="ExternalInput").ap()
    de_d = nc.dram_tensor("down_embed", [4096, DM], F32, kind="ExternalInput").ap()
    ue_d = nc.dram_tensor("up_embed", [4096, DM], F32, kind="ExternalInput").ap()
    pidx0_d = nc.dram_tensor("pidx0", [128, NT], I32, kind="ExternalInput").ap()
    pidx1_d = nc.dram_tensor("pidx1", [128, NT], I32, kind="ExternalInput").ap()
    kw_d = nc.dram_tensor("k_wT", [DM, 128], F32, kind="ExternalInput").ap()
    vw_d = nc.dram_tensor("v_wT", [DM, 128], F32, kind="ExternalInput").ap()
    qw_d = nc.dram_tensor("qw_r", [NH, NDC, 128, 128], F32, kind="ExternalInput").ap()
    ow_d = nc.dram_tensor("ow_r", [NDC, NH, 128, 128], F32, kind="ExternalInput").ap()
    kidx_d = nc.dram_tensor("kidx", [128, 4], I32, kind="ExternalInput").ap()
    vidx_d = nc.dram_tensor("vidx", [128, 32], I32, kind="ExternalInput").ap()
    rw_d = nc.dram_tensor("router_wT", [DM, 16], F32, kind="ExternalInput").ap()
    eb_d = nc.dram_tensor("e_bias", [1, 16], F32, kind="ExternalInput").ap()
    wup_d = nc.dram_tensor("w_up_r", [16, 8, NDC, 128, 128], F32, kind="ExternalInput").ap()
    wdn_d = nc.dram_tensor("w_down_r", [16, NDC, 8, 128, 128], F32, kind="ExternalInput").ap()
    shu_d = nc.dram_tensor("shup_r", [32, NDC, 128, 128], F32, kind="ExternalInput").ap()
    shd_d = nc.dram_tensor("shdown_r", [NDC, 32, 128, 128], F32, kind="ExternalInput").ap()
    out_d = nc.dram_tensor("out", [TPC, DM], F32, kind="ExternalOutput").ap()

    xn_tok_d = nc.dram_tensor("s_xn_tok", [TPC, DM], F32).ap()
    xnT_d = nc.dram_tensor("s_xnT", [DM, TPC], F32).ap()
    x1T_d = nc.dram_tensor("s_x1T", [DM, TPC], F32).ap()
    deltaT_d = nc.dram_tensor("s_deltaT", [DM, TPC], F32).ap()
    sinT_d = nc.dram_tensor("s_sinT", [DM, TPC], F32).ap()
    yT_d = nc.dram_tensor("s_yT", [DM, TPC], F32).ap()
    rin_t = nc.dram_tensor("s_rin", [TPC, 128], F32)
    rout_t = nc.dram_tensor("s_rout", [NCORES * TPC, 128], F32)
    stin_t = nc.dram_tensor("s_stin", [256, 256], F32)
    stout_t = nc.dram_tensor("s_stout", [NCORES * 256, 256], F32)
    kin_t = nc.dram_tensor("s_kin", [128, TPC], BF16)
    kg_t = nc.dram_tensor("s_kg", [NCORES * 128, TPC], BF16)
    vin_t = nc.dram_tensor("s_vin", [TPC, 128], BF16)
    vg_t = nc.dram_tensor("s_vg", [NCORES * TPC, 128], BF16)
    xn1T_d = nc.dram_tensor("s_xn1T", [DM, TPC], F32).ap()
    attnT_d = nc.dram_tensor("s_attnT", [DM, TPC], F32).ap()
    outT_d = nc.dram_tensor("s_outT", [DM, TPC], F32).ap()

    ctx = ExitStack()
    with ctx:
        p = Prog(nc, ctx)
        eT, eV, eS, eG = p.e["tensor"], p.e["vector"], p.e["scalar"], p.e["gpsimd"]

        ident = p.sbuf("g_ident", [128, 128])
        ones1f = p.sbuf("g_ones1f", [1, 128])
        onescf = p.sbuf("g_onescf", [128, 1])
        onesbf = p.sbuf("g_onesbf", [128, 1], BF16)
        iota16 = p.sbuf("g_iota16", [128, 16])
        iota64 = p.sbuf("g_iota64", [128, 64])
        esel = p.sbuf("g_esel", [16, DM])
        tsel = p.sbuf("g_tsel", [16, DM])
        scl = p.sbuf("g_scl", [128, 1])
        acol = p.sbuf("g_acol", [128, 256])
        dcol = p.sbuf("g_dcol", [128, NDC])
        bdt = p.sbuf("g_bdt", [128, NDC])
        alpha = p.sbuf("g_alpha", [128, 8])
        dlr_sb = p.sbuf("g_dlr", [64, TPC])
        bt_sb = p.sbuf("g_bt", [16, TPC])
        ct_sb = p.sbuf("g_ct", [16, TPC])
        hcar = p.sbuf("g_hcar", [128, 256])
        sincar = p.sbuf("g_sincar", [128, NDC])
        hin = p.sbuf("g_hin", [128, 256])
        ebias_bc = p.sbuf("g_ebias_bc", [128, 16])
        pidx0 = p.sbuf("g_pidx0", [128, NT], I32)
        pidx1 = p.sbuf("g_pidx1", [128, NT], I32)
        kidx = p.sbuf("g_kidx", [128, 4], I32)
        vidx = p.sbuf("g_vidx", [128, 32], I32)
        pb = [p.psum(f"g_pb{i}", [128, 512]) for i in range(7)]
        pbrow = p.psum("g_pbrow", [1, 512])
        ldA = p.sem("ldA")
        ldR = Ring(p, "ldR", 2)
        p.all_sems.extend(ldR.sems)
        ldC = p.sem("ldC")
        gqR = Ring(p, "gq", 2)
        p.all_sems.extend(gqR.sems)
        stA = p.sem("stA")
        stR = Ring(p, "stR", 2)
        p.all_sems.extend(stR.sems)
        gst = p.sem("gst")
        gstR = Ring(p, "gstR", 2)
        p.all_sems.extend(gstR.sems)
        qsR = Ring(p, "qsR", 2)
        p.all_sems.extend(qsR.sems)
        upS = p.sem("upS")
        dnS = p.sem("dnS")
        oS = p.sem("oS")
        cc = p.sem("cc")
        arena = Arena(nc)

        p.G(lambda e: e.memset(ident[:], 0.0))
        p.G(lambda e: e.affine_select(out=ident[:], in_=ident[:], compare_op=ALU.not_equal,
                                      fill=1.0, base=0, pattern=[[-1, 128]], channel_multiplier=1))
        p.G(lambda e: e.iota(iota16[:], pattern=[[1, 16]], base=0, channel_multiplier=0,
                             allow_small_or_imprecise_dtypes=True))
        p.G(lambda e: e.iota(iota64[:], pattern=[[1, 64]], base=0, channel_multiplier=0,
                             allow_small_or_imprecise_dtypes=True))
        p.G(lambda e: e.iota(esel[:].rearrange("k (n q) -> k n q", q=128),
                             pattern=[[1, 16], [0, 128]], base=0, channel_multiplier=0,
                             allow_small_or_imprecise_dtypes=True))
        gio = p.G(lambda e: e.iota(tsel[:], pattern=[[0, DM]], base=0, channel_multiplier=1,
                                   allow_small_or_imprecise_dtypes=True))
        p.wait("vector", eG, gio)
        p.V(lambda e: e.tensor_tensor(esel[:], esel[:], tsel[:], ALU.is_equal))
        p.V(lambda e: e.memset(ones1f[:], 1.0))
        p.V(lambda e: e.memset(onescf[:], 1.0))
        p.V(lambda e: e.memset(onesbf[:], 1.0))
        p.V(lambda e: e.memset(scl[:], float(1.0 / np.sqrt(128.0))))
        p.V(lambda e: e.memset(hcar[:], 0.0))
        p.V(lambda e: e.memset(sincar[:], 0.0))
        p.dma("sync", bdt[:], bdt_d[:], ldA)
        p.dma("sync", acol[:], alog_d[:], ldA)
        p.dma("sync", dcol[:], dr_d[:], ldA)
        p.dma("sync", alpha[:], alpha_d[:], ldA)
        p.dma("sync", pidx0[:], pidx0_d[:], ldA)
        p.dma("sync", pidx1[:], pidx1_d[:], ldA)
        p.dma("sync", kidx[:], kidx_d[:], ldA)
        p.dma("sync", vidx[:], vidx_d[:], ldA)
        p.dma("sync", ebias_bc[0:1, :], eb_d[:], ldA)
        ldi = ldA.n
        p.wait("scalar", ldA, ldi)
        aev = p.S(lambda e: e.activation(acol[:], acol[:], AF.Exp))
        p.wait("vector", eS, aev)
        p.V(lambda e: e.tensor_single_scalar(acol[:], acol[:], -1.0, ALU.mult))
        p.wait("tensor", ldA, ldi)
        p.wait("tensor", eV, p.e["vector"].n)
        bm = p.T(lambda e: e.matmul(pb[0][:, 0:16], ones1f[:], ebias_bc[0:1, :], start=True, stop=True))
        p.wait("vector", eT, bm)
        p.V(lambda e: e.tensor_copy(ebias_bc[:], pb[0][:, 0:16]))
        p.barrier()

        arena.start()
        s1_xt = [arena("s1_xt", [128, DM]) for _ in range(2)]
        s1_ot = [arena("s1_ot", [128, DM]) for _ in range(2)]
        s1_junk = arena("s1_junk", [128, DM])
        s1_s1 = arena("s1_s1", [128, NT])
        s1_s2 = arena("s1_s2", [128, NT])
        s1_mean = arena("s1_mean", [128, NT])
        s1_var = arena("s1_var", [128, NT])
        s1_rstd = arena("s1_rstd", [128, NT])
        s1_cinv = arena("s1_cinv", [128, 1])
        x_img = arena("s1_ximg", [128, NDC * TPC])
        xn_img = arena("s1_xnimg", [128, NDC * TPC])
        p.V(lambda e: e.memset(s1_cinv[:], 1.0 / DM))
        tile_T = {}
        tile_store = {}
        for i in range(NT):
            b = i % 2
            xs = s1_xt[b]
            lds = ldR[i]
            rows = slice(i * 128, (i + 1) * 128)
            if i >= 2:
                p.wait("sync", eT, tile_T[i - 2])
            ldn = p.dma("sync", xs[:], x_d[rows, :], lds)
            p.wait("vector", lds, ldn)
            p.wait("scalar", lds, ldn)
            si = slice(i, i + 1)
            p.V(lambda e, xs=xs, si=si: e.reduce_sum(s1_s1[:, si], xs[:], axis=AX.X))
            a2 = p.S(lambda e, xs=xs, si=si: e.activation(s1_junk[:], xs[:], AF.Square, accum_out=s1_s2[:, si]))
            p.wait("vector", eS, a2)
            p.V(lambda e, si=si: e.tensor_scalar(s1_mean[:, si], s1_s1[:, si], s1_cinv[:], None, ALU.mult))
            p.V(lambda e, si=si: e.tensor_tensor(s1_var[:, si], s1_mean[:, si], s1_mean[:, si], ALU.mult))
            p.V(lambda e, si=si: e.scalar_tensor_tensor(s1_var[:, si], s1_s2[:, si], s1_cinv[:], s1_var[:, si], ALU.mult, ALU.subtract))
            vv = p.V(lambda e, si=si: e.tensor_single_scalar(s1_var[:, si], s1_var[:, si], 1e-5, ALU.add))
            p.wait("scalar", eV, vv)
            asq = p.S(lambda e, si=si: e.activation(s1_var[:, si], s1_var[:, si], AF.Sqrt))
            p.wait("vector", eS, asq)
            p.V(lambda e, si=si: e.reciprocal(s1_rstd[:, si], s1_var[:, si]))
            p.wait("vector", eV, eV.n)
            if i >= 2:
                p.wait("vector", *tile_store[i - 2])
            ob = s1_ot[b]
            vn = p.V(lambda e, xs=xs, ob=ob, i=i: e.tensor_scalar(
                ob[:], xs[:], s1_mean[:, i:i + 1], s1_rstd[:, i:i + 1], ALU.subtract, ALU.mult))
            p.wait("sync", eV, vn)
            p.dma("sync", xn_tok_d[rows, :], ob[:], stR[i])
            tile_store[i] = (stR[i], stR[i].n)
            for g4 in range(4):
                p.wait("tensor", eV, vn)
                for j in range(4):
                    c = g4 * 4 + j
                    p.T(lambda e, xs=xs, c=c, j=j: e.transpose(
                        pb[0][:, j * 128:(j + 1) * 128], xs[:, c * 128:(c + 1) * 128], ident[:]))
                    p.T(lambda e, ob=ob, c=c, j=j: e.transpose(
                        pb[1][:, j * 128:(j + 1) * 128], ob[:, c * 128:(c + 1) * 128], ident[:]))
                tn = p.e["tensor"].n
                p.wait("vector", eT, tn)
                p.V(lambda e, g4=g4, i=i: e.tensor_copy(
                    x_img[:].rearrange("p (c t) -> p c t", c=NDC)[:, g4 * 4:(g4 + 1) * 4, i * 128:(i + 1) * 128],
                    pb[0][:, 0:512].rearrange("p (j t) -> p j t", j=4)))
                ev = p.V(lambda e, g4=g4, i=i: e.tensor_copy(
                    xn_img[:].rearrange("p (c t) -> p c t", c=NDC)[:, g4 * 4:(g4 + 1) * 4, i * 128:(i + 1) * 128],
                    pb[1][:, 0:512].rearrange("p (j t) -> p j t", j=4)))
                p.wait("tensor", eV, ev)
            tile_T[i] = p.e["tensor"].n
        p.wait("sync", eV, eV.n)
        p.dma("sync", fm_dram_ap(x1T_d, NDC), x_img[:].rearrange("p (c t) -> p c t", c=NDC), stA)
        p.dma("sync", fm_dram_ap(xnT_d, NDC), xn_img[:].rearrange("p (c t) -> p c t", c=NDC), stA)
        p.barrier()

        arena.start()
        s2_xn = arena("s2_xn", [128, NDC * TPC])
        s2_win = arena("s2_win", [128, NDC * 96])
        s2_wdt = arena("s2_wdt", [64, DM])
        s2_wq = arena("s2_wq", [128, NDC * 128])
        s2_keys = arena("s2_keys", [128, 64])
        s2_q = arena("s2_q", [128, TPC])
        s2_v = [arena("s2_v", [128, 512]) for _ in range(2)]
        s2_r = [arena("s2_r", [128, 512]) for _ in range(2)]
        s2_w = [arena("s2_w", [128, 512]) for _ in range(2)]
        s2_el = [arena("s2_el", [128, 512]) for _ in range(2)]
        s2_dl = [arena("s2_dl", [128, 512]) for _ in range(2)]
        s2_one = arena("s2_one", [128, 1])
        s2_rt = arena("s2_rt", [128, 128])
        p.V(lambda e: e.memset(s2_one[:], 1.0))
        fm_load(p, s2_xn, xnT_d, NDC, ldA)
        fm_load(p, s2_win, winT_d, NDC, ldA)
        fm_load(p, s2_wq, wq_d, NDC, ldA)
        p.dma("sync", s2_wdt[:], wdtT_d[:], ldA)
        p.dma("sync", s2_keys[0:64, :], keys_d[:], ldA)
        p.dma("sync", s2_keys[64:128, :], keys_d[:], ldA)
        ld2 = ldA.n
        p.wait("tensor", ldA, ld2)
        p.wait("vector", ldA, ld2)
        p.wait("scalar", ldA, ld2)
        for th in range(2):
            tsl = slice(th * 512, (th + 1) * 512)
            for gi, (o, m, dst) in enumerate([(0, 64, dlr_sb), (64, 16, bt_sb), (80, 16, ct_sb)]):
                mmn = None
                for c in range(NDC):
                    mmn = p.T(lambda e, c=c, o=o, m=m, th=th, gi=gi: e.matmul(
                        pb[gi][0:m, 0:512], s2_win[:, c * 96 + o: c * 96 + o + m],
                        s2_xn[:, c * TPC + th * 512: c * TPC + th * 512 + 512],
                        start=(c == 0), stop=(c == NDC - 1)))
                p.wait("vector", eT, mmn)
                ev = p.V(lambda e, m=m, dst=dst, tsl=tsl, gi=gi: e.tensor_copy(
                    dst[0:m, tsl], pb[gi][0:m, 0:512]))
                p.wait("tensor", eV, ev)
        store_done = {}
        psum_free = {}
        for c in range(NDC):
            for th in range(2):
                k = c * 2 + th
                par = k % 2
                ps = pb[par]
                if k >= 2:
                    p.wait("tensor", eV, psum_free[k - 2])
                mmn = p.T(lambda e, ps=ps, c=c, th=th: e.matmul(
                    ps[:, :], s2_wdt[:, c * 128:(c + 1) * 128],
                    dlr_sb[:, th * 512:(th + 1) * 512], start=True, stop=True))
                p.wait("vector", eT, mmn)
                p.V(lambda e, ps=ps, c=c, par=par: e.tensor_scalar(
                    s2_v[par][:], ps[:, :], bdt[:, c:c + 1], None, ALU.add))
                psum_free[k] = p.V(lambda e, par=par: e.tensor_single_scalar(
                    s2_r[par][:], s2_v[par][:], 0.0, ALU.max))
                wv = p.V(lambda e, par=par: e.scalar_tensor_tensor(
                    s2_w[par][:], s2_r[par][:], -2.0, s2_v[par][:], ALU.mult, ALU.add))
                p.wait("scalar", eV, wv)
                p.S(lambda e, par=par: e.activation(s2_el[par][:], s2_w[par][:], AF.Exp))
                lnn = p.S(lambda e, par=par: e.activation(s2_el[par][:], s2_el[par][:], AF.Ln, bias=s2_one[:]))
                if k >= 2:
                    p.wait("vector", *store_done[k - 2])
                p.wait("vector", eS, lnn)
                dl = p.V(lambda e, par=par: e.tensor_tensor(s2_dl[par][:], s2_r[par][:], s2_el[par][:], ALU.add))
                p.wait("sync", eV, dl)
                sts = stR[k]
                p.dma("sync", deltaT_d[c * 128:(c + 1) * 128, th * 512:(th + 1) * 512], s2_dl[par][:], sts)
                store_done[k] = (sts, sts.n)
        for th in range(2):
            mmn = None
            for c in range(NDC):
                mmn = p.T(lambda e, th=th, c=c: e.matmul(
                    pb[2 + th][:, :], s2_wq[:, c * 128:(c + 1) * 128],
                    s2_xn[:, c * TPC + th * 512: c * TPC + th * 512 + 512],
                    start=(c == 0), stop=(c == NDC - 1)))
            p.wait("vector", eT, mmn)
            ev = p.V(lambda e, th=th: e.tensor_copy(s2_q[:, th * 512:(th + 1) * 512], pb[2 + th][:, :]))
            p.wait("tensor", eV, ev)
        for i in range(NT):
            msl = slice(i * 128, (i + 1) * 128)
            p.wait("vector", stA, stA.n)
            m0 = p.T(lambda e, msl=msl: e.matmul(pb[4][:, 0:64], s2_q[0:64, msl], s2_keys[0:64, :], start=True, stop=True))
            m1 = p.T(lambda e, msl=msl: e.matmul(pb[4][:, 64:128], s2_q[64:128, msl], s2_keys[64:128, :], start=True, stop=True))
            p.wait("vector", eT, m1)
            rv = p.V(lambda e: e.tensor_copy(s2_rt[:], pb[4][:, 0:128]))
            p.wait("tensor", eV, rv)
            p.wait("sync", eV, rv)
            p.dma("sync", rin_t.ap()[msl, :], s2_rt[:], stA)
        p.barrier()
        p.coll("AllGather", [rin_t.ap().opt()], [rout_t.ap().opt()], cc)

        arena.start()
        s3_delta = arena("s3_delta", [128, NDC * 512])
        s3_dx = arena("s3_dx", [128, NDC * 512])
        s3_y = arena("s3_y", [128, NDC * 512])
        s3_sin = arena("s3_sin", [128, NDC * 512])
        s3_bbc = arena("s3_bbc", [128, 512])
        s3_cbc = arena("s3_cbc", [128, 512])
        s3_a = arena("s3_a", [128, 512])
        s3_b = arena("s3_b", [128, 512])
        s3_h = arena("s3_h", [128, 512])
        s3_tmp = arena("s3_tmp", [128, 512])
        s3_p = arena("s3_p", [128, 256])
        for hf in range(2):
            tsl = slice(hf * 512, (hf + 1) * 512)
            p.dma("sync", s3_delta[:].rearrange("p (c t) -> p c t", c=NDC),
                  deltaT_d.rearrange("(c p) t -> p c t", p=128)[:, :, tsl], ldA)
            ld3 = p.dma("sync", s3_dx[:].rearrange("p (c t) -> p c t", c=NDC),
                        xnT_d.rearrange("(c p) t -> p c t", p=128)[:, :, tsl], ldA)
            p.wait("vector", ldA, ld3)
            p.wait("scalar", ldA, ld3)
            for c in range(NDC):
                p.V(lambda e, c=c: e.tensor_scalar(
                    s3_y[:, c * 512:(c + 1) * 512], s3_dx[:, c * 512:(c + 1) * 512],
                    dcol[:, c:c + 1], None, ALU.mult))
                p.V(lambda e, c=c: e.tensor_tensor(
                    s3_dx[:, c * 512:(c + 1) * 512], s3_delta[:, c * 512:(c + 1) * 512],
                    s3_dx[:, c * 512:(c + 1) * 512], ALU.mult))
                p.V(lambda e, c=c: e.tensor_tensor_scan(
                    s3_sin[:, c * 512:(c + 1) * 512], s3_delta[:, c * 512:(c + 1) * 512],
                    s3_delta[:, c * 512:(c + 1) * 512], sincar[:, c:c + 1], ALU.add, ALU.bypass))
                p.V(lambda e, c=c: e.tensor_copy(sincar[:, c:c + 1], s3_sin[:, c * 512 + 511: c * 512 + 512]))
            prep_v = p.e["vector"].n
            p.wait("tensor", eV, prep_v)
            p.wait("scalar", eV, prep_v)
            for n in range(16):
                m1 = p.T(lambda e, n=n, tsl=tsl: e.matmul(
                    pb[5][:, :], esel[:, n * 128:(n + 1) * 128], bt_sb[:, tsl], start=True, stop=True))
                m2 = p.T(lambda e, n=n, tsl=tsl: e.matmul(
                    pb[6][:, :], esel[:, n * 128:(n + 1) * 128], ct_sb[:, tsl], start=True, stop=True))
                p.wait("vector", eT, m2)
                p.V(lambda e: e.tensor_copy(s3_bbc[:], pb[5][:, :]))
                bcv = p.V(lambda e: e.tensor_copy(s3_cbc[:], pb[6][:, :]))
                p.wait("tensor", eV, bcv)
                for Dc in range(NDC):
                    col = Dc * 16 + n
                    dsl = slice(Dc * 512, (Dc + 1) * 512)
                    p.wait("scalar", eV, p.e["vector"].n)
                    an = p.S(lambda e, dsl=dsl, col=col: e.activation(
                        s3_a[:], s3_delta[:, dsl], AF.Exp, scale=acol[:, col:col + 1]))
                    p.V(lambda e, dsl=dsl: e.tensor_tensor(s3_b[:], s3_dx[:, dsl], s3_bbc[:], ALU.mult))
                    p.wait("vector", eS, an)
                    p.V(lambda e, col=col: e.tensor_tensor_scan(
                        s3_h[:], s3_a[:], s3_b[:], hcar[:, col:col + 1], ALU.mult, ALU.add))
                    p.V(lambda e, col=col: e.tensor_copy(hcar[:, col:col + 1], s3_h[:, 511:512]))
                    p.V(lambda e: e.tensor_tensor(s3_tmp[:], s3_h[:], s3_cbc[:], ALU.mult))
                    p.V(lambda e, dsl=dsl: e.tensor_tensor(s3_y[:, dsl], s3_y[:, dsl], s3_tmp[:], ALU.add))
            yv = p.e["vector"].n
            p.wait("sync", eV, yv)
            p.dma("sync", yT_d.rearrange("(c p) t -> p c t", p=128)[:, :, tsl],
                  s3_y[:].rearrange("p (c t) -> p c t", c=NDC), stA)
            p.dma("sync", sinT_d.rearrange("(c p) t -> p c t", p=128)[:, :, tsl],
                  s3_sin[:].rearrange("p (c t) -> p c t", c=NDC), stA)
            if hf == 0:
                p.wait("vector", stA, stA.n)
                p.wait("sync", stA, stA.n)
        for Dc in range(NDC):
            p.V(lambda e, Dc=Dc: e.tensor_tensor(
                s3_p[:, Dc * 16:(Dc + 1) * 16],
                sincar[:, Dc:Dc + 1].to_broadcast([128, 16]),
                acol[:, Dc * 16:(Dc + 1) * 16], ALU.mult))
        pv = p.e["vector"].n
        p.wait("scalar", eV, pv)
        pe_ = p.S(lambda e: e.activation(s3_p[:], s3_p[:], AF.Exp))
        p.wait("sync", eS, pe_)
        p.dma("sync", stin_t.ap()[0:128, :], s3_p[:], stA)
        p.dma("sync", stin_t.ap()[128:256, :], hcar[:], stA)
        p.barrier()
        p.coll("AllGather", [stin_t.ap().opt()], [stout_t.ap().opt()], cc)

        arena.start()
        m_xnb = arena("m_xnb", [128, NDC * TPC], BF16)
        m_gsl = [arena("m_gsl", [128, NDC * 128], BF16) for _ in range(2)]
        m_usl = [arena("m_usl", [128, NDC * 128], BF16) for _ in range(2)]
        m_dsl = [arena("m_dsl", [128, 64 * 128], BF16) for _ in range(2)]
        m_act = arena("m_act", [128, 64 * 512], BF16)
        m_sg = arena("m_sg", [128, 512])
        m_mo = [arena("m_mo", [128, 512]) for _ in range(2)]
        p.dma("gpsimd", m_xnb[:].rearrange("p (c t) -> p c t", c=NDC), fm_dram_ap(xnT_d, NDC), ldC)
        p.wait("tensor", ldC, ldC.n)
        gu_done = {}
        down_mm = {}
        acc_done = {}
        nacc = 0
        for th in range(2):
            tsl = slice(th * 512, (th + 1) * 512)
            for ic in range(64):
                gi = th * 64 + ic
                par = gi % 2
                lds = gqR[gi]
                if gi >= 2:
                    p.wait("gpsimd", eV, gu_done[gi - 2])
                p.dma("gpsimd", m_gsl[par][:].rearrange("p (c i) -> p c i", c=NDC),
                      gate_d[ic].rearrange("c p i -> p c i"), lds)
                ldn = p.dma("gpsimd", m_usl[par][:].rearrange("p (c i) -> p c i", c=NDC),
                            up_d[ic].rearrange("c p i -> p c i"), lds)
                p.wait("tensor", lds, ldn)
                for c in range(NDC):
                    p.T(lambda e, par=par, c=c, tsl=tsl: e.matmul(
                        pb[0][:, :], m_gsl[par][:, c * 128:(c + 1) * 128],
                        m_xnb[:, c * TPC + tsl.start: c * TPC + tsl.stop], start=(c == 0), stop=(c == NDC - 1)))
                mmu = None
                for c in range(NDC):
                    mmu = p.T(lambda e, par=par, c=c, tsl=tsl: e.matmul(
                        pb[1][:, :], m_usl[par][:, c * 128:(c + 1) * 128],
                        m_xnb[:, c * TPC + tsl.start: c * TPC + tsl.stop], start=(c == 0), stop=(c == NDC - 1)))
                p.wait("scalar", eT, mmu)
                sgn = p.S(lambda e: e.activation(m_sg[:], pb[0][:, :], AF.Silu))
                p.wait("vector", eS, sgn)
                gu_done[gi] = p.V(lambda e, ic=ic: e.tensor_tensor(
                    m_act[:, ic * 512:(ic + 1) * 512], m_sg[:], pb[1][:, :], ALU.mult))
                p.wait("tensor", eV, gu_done[gi])
            for dc in range(NDC):
                gj = th * NDC + dc
                par = gj % 2
                lds = dnS
                if gj >= 2:
                    p.wait("gpsimd", eT, down_mm[gj - 2])
                ldn = p.dma("gpsimd", m_dsl[par][:].rearrange("p (c d) -> p c d", c=64),
                            down_d[dc].rearrange("c p d -> p c d"), lds, serial=True)
                p.wait("tensor", lds, ldn)
                mmn = None
                for ic in range(64):
                    mmn = p.T(lambda e, par=par, ic=ic, gj=gj: e.matmul(
                        pb[2 + gj % 2][:, :], m_dsl[par][:, ic * 128:(ic + 1) * 128],
                        m_act[:, ic * 512:(ic + 1) * 512], start=(ic == 0), stop=(ic == 63)))
                down_mm[gj] = mmn
                p.wait("vector", eT, mmn)
                if nacc >= 2:
                    p.wait("vector", *acc_done[nacc - 2])
                mon = p.V(lambda e, gj=gj: e.tensor_copy(m_mo[gj % 2][:], pb[2 + gj % 2][:, :]))
                p.wait("tensor", eV, mon)
                p.wait("gpsimd", eV, mon)
                p.dma_ex("gpsimd", x1T_d[dc * 128:(dc + 1) * 128, tsl], m_mo[gj % 2][:], gstR[nacc], accum=True)
                acc_done[nacc] = (gstR[nacc], gstR[nacc].n)
                nacc += 1
            p.wait("vector", eT, p.e["tensor"].n)
        p.barrier()

        arena.start()
        c_pj = arena("c_pj", [128, 256])
        c_hj = arena("c_hj", [128, 256])
        c_t = arena("c_t", [128, 256])
        c_y = arena("c_y", [128, NDC * 512])
        c_sin = arena("c_sin", [128, NDC * 512])
        c_cbc = arena("c_cbc", [128, 512])
        c_e = arena("c_e", [128, 512])
        c_tm = arena("c_tm", [128, 512])
        p.V(lambda e: e.memset(hin[:], 0.0))
        p.wait("sync", cc, cc.n)
        for j in range(NCORES):
            ldn = p.dma("sync", c_pj[:], stout_t.ap()[j * 256: j * 256 + 128, :], ldA)
            ldn = p.dma("sync", c_hj[:], stout_t.ap()[j * 256 + 128: (j + 1) * 256, :], ldA)
            p.wait("vector", ldA, ldn)
            p.V(lambda e: e.tensor_single_scalar(c_t[:], c_pj[:], -1.0, ALU.add))
            p.V(lambda e, j=j: e.tensor_scalar(c_t[:], c_t[:], alpha[:, j:j + 1], None, ALU.mult))
            p.V(lambda e: e.tensor_single_scalar(c_t[:], c_t[:], 1.0, ALU.add))
            p.V(lambda e: e.tensor_tensor(hin[:], c_t[:], hin[:], ALU.mult))
            p.V(lambda e, j=j: e.tensor_scalar(c_t[:], c_hj[:], alpha[:, j:j + 1], None, ALU.mult))
            hv = p.V(lambda e: e.tensor_tensor(hin[:], hin[:], c_t[:], ALU.add))
            p.wait("sync", eV, hv)
        for hf in range(2):
            tsl = slice(hf * 512, (hf + 1) * 512)
            p.dma("sync", c_y[:].rearrange("p (c t) -> p c t", c=NDC),
                  yT_d.rearrange("(c p) t -> p c t", p=128)[:, :, tsl], ldA)
            ld5 = p.dma("sync", c_sin[:].rearrange("p (c t) -> p c t", c=NDC),
                        sinT_d.rearrange("(c p) t -> p c t", p=128)[:, :, tsl], ldA)
            p.wait("vector", ldA, ld5)
            p.wait("scalar", ldA, ld5)
            p.wait("tensor", ldA, ld5)
            for n in range(16):
                m2 = p.T(lambda e, n=n, tsl=tsl: e.matmul(
                    pb[6][:, :], esel[:, n * 128:(n + 1) * 128], ct_sb[:, tsl], start=True, stop=True))
                p.wait("vector", eT, m2)
                bcv = p.V(lambda e: e.tensor_copy(c_cbc[:], pb[6][:, :]))
                p.wait("tensor", eV, bcv)
                for Dc in range(NDC):
                    col = Dc * 16 + n
                    dsl = slice(Dc * 512, (Dc + 1) * 512)
                    p.wait("scalar", eV, p.e["vector"].n)
                    an = p.S(lambda e, dsl=dsl, col=col: e.activation(
                        c_e[:], c_sin[:, dsl], AF.Exp, scale=acol[:, col:col + 1]))
                    p.wait("vector", eS, an)
                    p.V(lambda e, col=col: e.scalar_tensor_tensor(
                        c_tm[:], c_e[:], hin[:, col:col + 1], c_cbc[:], ALU.mult, ALU.mult))
                    p.V(lambda e, dsl=dsl: e.tensor_tensor(c_y[:, dsl], c_y[:, dsl], c_tm[:], ALU.add))
            yv = p.e["vector"].n
            p.wait("gpsimd", eV, yv)
            p.dma_ex("gpsimd", x1T_d.rearrange("(c p) t -> p c t", p=128)[:, :, tsl],
                     c_y[:].rearrange("p (c t) -> p c t", c=NDC), gst, accum=True)
            if hf == 0:
                p.wait("vector", gst, gst.n)
                p.wait("sync", gst, gst.n)
        p.barrier()


        arena.start()
        p_xt = [arena("p_xt", [128, DM]) for _ in range(2)]
        p_yt = [arena("p_yt", [128, DM]) for _ in range(2)]
        p_ga = arena("p_ga", [128, 8 * DM])
        p_yimg = arena("p_yimg", [128, NDC * TPC])
        p_r0 = arena("p_r0", [128, 64])
        p_r1 = arena("p_r1", [128, 64])
        p_junk64 = arena("p_junk64", [128, 64])
        p_junk2k = arena("p_junk2k", [128, DM])
        p_sx8 = arena("p_sx8", [128, 8])
        p_sy8 = arena("p_sy8", [128, 8])
        p_ix8 = arena("p_ix8", [128, 8])
        p_iy8 = arena("p_iy8", [128, 8])
        p_alls = arena("p_alls", [128, 64])
        p_alli = arena("p_alli", [128, 64])
        p_sc8 = arena("p_sc8", [128, 8])
        p_fidx = arena("p_fidx", [128, 8])
        p_idx32 = arena("p_idx32", [128, 8], I32)
        p_negmax = arena("p_negmax", [128, 1])
        p_pexp = arena("p_pexp", [128, 8])
        p_ssum = arena("p_ssum", [128, 1])
        p_rsum = arena("p_rsum", [128, 1])
        p_ew8 = arena("p_ew8", [128, 8])
        p_sil8 = arena("p_sil8", [128, 8])
        p_w8 = arena("p_w8", [128, 8])
        rout2 = rout_t.ap().rearrange("a (b c) -> (a b) c", b=2)
        p.wait("vector", cc, 1)
        p.wait("gpsimd", cc, 1)
        tile_dots = {}
        for i in range(NT):
            par = i % 2
            lds = ldR[i]
            rows = slice(i * 128, (i + 1) * 128)
            if i >= 2:
                p.wait("sync", eV, tile_dots[i - 2])
            ldn = p.dma("sync", p_xt[par][:], xn_tok_d[rows, :], lds)
            g1 = p.idma(p_r0[:], rout2, pidx0[:, i:i + 1], 0, gqR[i])
            g1 = p.idma(p_r1[:], rout2, pidx1[:, i:i + 1], 0, gqR[i])
            p.wait("vector", gqR[i], g1)
            p.V(lambda e: e.max(out=p_sx8[:], in_=p_r0[:]))
            for cq in range(8):
                p.V(lambda e, cq=cq: e.scalar_tensor_tensor(
                    p_junk64[:], p_r0[:], p_sx8[:, cq:cq + 1], iota64[:], ALU.is_equal, ALU.mult,
                    accum_out=p_ix8[:, cq:cq + 1]))
            p.V(lambda e: e.max(out=p_sy8[:], in_=p_r1[:]))
            for cq in range(8):
                p.V(lambda e, cq=cq: e.scalar_tensor_tensor(
                    p_junk64[:], p_r1[:], p_sy8[:, cq:cq + 1], iota64[:], ALU.is_equal, ALU.mult,
                    accum_out=p_iy8[:, cq:cq + 1]))
            p.V(lambda e: e.tensor_tensor(
                p_alls[:].rearrange("p (a c) -> p a c", c=8),
                p_sx8[:].rearrange("p (k o) -> p k o", o=1).to_broadcast([128, 8, 8]),
                p_sy8[:].rearrange("p (o k) -> p o k", o=1).to_broadcast([128, 8, 8]), ALU.add))
            p.V(lambda e: e.tensor_single_scalar(p_ix8[:], p_ix8[:], 64.0, ALU.mult))
            p.V(lambda e: e.tensor_tensor(
                p_alli[:].rearrange("p (a c) -> p a c", c=8),
                p_ix8[:].rearrange("p (k o) -> p k o", o=1).to_broadcast([128, 8, 8]),
                p_iy8[:].rearrange("p (o k) -> p o k", o=1).to_broadcast([128, 8, 8]), ALU.add))
            p.V(lambda e: e.max(out=p_sc8[:], in_=p_alls[:]))
            for cq in range(8):
                p.V(lambda e, cq=cq: e.scalar_tensor_tensor(
                    p_junk64[:], p_alls[:], p_sc8[:, cq:cq + 1], p_alli[:], ALU.is_equal, ALU.mult,
                    accum_out=p_fidx[:, cq:cq + 1]))
            p.V(lambda e: e.tensor_single_scalar(p_fidx[:], p_fidx[:], 4095.0, ALU.min))
            p.V(lambda e: e.tensor_single_scalar(p_negmax[:], p_sc8[:, 0:1], -1.0, ALU.mult))
            vneg = p.e["vector"].n
            p.wait("scalar", eV, vneg)
            aexp = p.S(lambda e: e.activation(p_pexp[:], p_sc8[:], AF.Exp, bias=p_negmax[:], accum_out=p_ssum[:]))
            p.wait("vector", eS, aexp)
            p.V(lambda e: e.reciprocal(p_rsum[:], p_ssum[:]))
            vidxv = p.V(lambda e: e.tensor_copy(p_idx32[:], p_fidx[:]))
            p.wait("gpsimd", eV, vidxv)
            gn = None
            for kq in range(8):
                gn = p.idma(p_ga[:, kq * DM:(kq + 1) * DM], de_d[:, :], p_idx32[:, kq:kq + 1], 0, gqR[i])
            p.wait("vector", lds, ldn)
            p.wait("vector", gqR[i], gn)
            for kq in range(8):
                p.V(lambda e, kq=kq, par=par: e.scalar_tensor_tensor(
                    p_junk2k[:], p_ga[:, kq * DM:(kq + 1) * DM], 0.0, p_xt[par][:], ALU.bypass, ALU.mult,
                    accum_out=p_ew8[:, kq:kq + 1]))
            dots_v = p.e["vector"].n
            tile_dots[i] = dots_v
            p.wait("scalar", eV, dots_v)
            asil = p.S(lambda e: e.activation(p_sil8[:], p_ew8[:], AF.Silu))
            p.wait("vector", eS, asil)
            p.V(lambda e: e.scalar_tensor_tensor(p_w8[:], p_sil8[:], p_rsum[:], p_pexp[:], ALU.mult, ALU.mult))
            p.wait("gpsimd", eV, dots_v)
            gn2 = None
            for kq in range(8):
                gn2 = p.idma(p_ga[:, kq * DM:(kq + 1) * DM], ue_d[:, :], p_idx32[:, kq:kq + 1], 0, gqR[i])
            p.wait("vector", gqR[i], gn2)
            p.V(lambda e, par=par: e.tensor_scalar(
                p_yt[par][:], p_ga[:, 0:DM], p_w8[:, 0:1], None, ALU.mult))
            for kq in range(1, 8):
                p.V(lambda e, kq=kq, par=par: e.scalar_tensor_tensor(
                    p_yt[par][:], p_ga[:, kq * DM:(kq + 1) * DM], p_w8[:, kq:kq + 1], p_yt[par][:], ALU.mult, ALU.add))
            yv = p.e["vector"].n
            for g4 in range(4):
                p.wait("tensor", eV, yv)
                for j in range(4):
                    cch = g4 * 4 + j
                    p.T(lambda e, par=par, cch=cch, j=j: e.transpose(
                        pb[0][:, j * 128:(j + 1) * 128], p_yt[par][:, cch * 128:(cch + 1) * 128], ident[:]))
                tn = p.e["tensor"].n
                p.wait("vector", eT, tn)
                ev = p.V(lambda e, g4=g4, i=i: e.tensor_copy(
                    p_yimg[:].rearrange("p (c t) -> p c t", c=NDC)[:, g4 * 4:(g4 + 1) * 4, i * 128:(i + 1) * 128],
                    pb[0][:, 0:512].rearrange("p (j t) -> p j t", j=4)))
                p.wait("tensor", eV, ev)
        p.wait("gpsimd", eV, p.e["vector"].n)
        p.dma_ex("gpsimd", fm_dram_ap(x1T_d, NDC), p_yimg[:].rearrange("p (c t) -> p c t", c=NDC), gst, accum=True)
        p.barrier()

        arena.start()
        l_x1 = arena("l_x1", [128, NDC * TPC])
        l_xn1 = arena("l_xn1", [128, NDC * TPC])
        l_xt = arena("l_xt", [128, DM])
        l_ot = arena("l_ot", [128, DM])
        l_junk = arena("l_junk", [128, DM])
        l_s1 = arena("l_s1", [128, NT])
        l_s2 = arena("l_s2", [128, NT])
        l_mean = arena("l_mean", [128, NT])
        l_var = arena("l_var", [128, NT])
        l_rstd = arena("l_rstd", [128, NT])
        l_cinv = arena("l_cinv", [128, 1])
        p.V(lambda e: e.memset(l_cinv[:], 1.0 / DM))
        ld7 = None
        fm_load(p, l_x1, x1T_d, NDC, ldA)
        ld7 = ldA.n
        p.wait("tensor", ldA, ld7)
        for i in range(NT):
            for g4 in range(4):
                if i > 0 or g4 > 0:
                    p.wait("tensor", eV, p.e["vector"].n)
                for j in range(4):
                    cch = g4 * 4 + j
                    p.T(lambda e, cch=cch, j=j, i=i: e.transpose(
                        pb[0][:, j * 128:(j + 1) * 128],
                        l_x1[:, cch * TPC + i * 128: cch * TPC + (i + 1) * 128], ident[:]))
                tn = p.e["tensor"].n
                p.wait("vector", eT, tn)
                p.V(lambda e, g4=g4: e.tensor_copy(
                    l_xt[:, g4 * 512:(g4 + 1) * 512], pb[0][:, 0:512]))
            xv = p.e["vector"].n
            p.wait("scalar", eV, xv)
            si = slice(i, i + 1)
            p.V(lambda e, si=si: e.reduce_sum(l_s1[:, si], l_xt[:], axis=AX.X))
            a2 = p.S(lambda e, si=si: e.activation(l_junk[:], l_xt[:], AF.Square, accum_out=l_s2[:, si]))
            p.wait("vector", eS, a2)
            p.V(lambda e, si=si: e.tensor_scalar(l_mean[:, si], l_s1[:, si], l_cinv[:], None, ALU.mult))
            p.V(lambda e, si=si: e.tensor_tensor(l_var[:, si], l_mean[:, si], l_mean[:, si], ALU.mult))
            p.V(lambda e, si=si: e.scalar_tensor_tensor(l_var[:, si], l_s2[:, si], l_cinv[:], l_var[:, si], ALU.mult, ALU.subtract))
            vv = p.V(lambda e, si=si: e.tensor_single_scalar(l_var[:, si], l_var[:, si], 1e-5, ALU.add))
            p.wait("scalar", eV, vv)
            asq = p.S(lambda e, si=si: e.activation(l_var[:, si], l_var[:, si], AF.Sqrt))
            p.wait("vector", eS, asq)
            p.V(lambda e, si=si: e.reciprocal(l_rstd[:, si], l_var[:, si]))
            p.wait("vector", eV, eV.n)
            vn = p.V(lambda e, i=i: e.tensor_scalar(
                l_ot[:], l_xt[:], l_mean[:, i:i + 1], l_rstd[:, i:i + 1], ALU.subtract, ALU.mult))
            for g4 in range(4):
                p.wait("tensor", eV, vn if g4 == 0 else p.e["vector"].n)
                for j in range(4):
                    cch = g4 * 4 + j
                    p.T(lambda e, cch=cch, j=j: e.transpose(
                        pb[1][:, j * 128:(j + 1) * 128], l_ot[:, cch * 128:(cch + 1) * 128], ident[:]))
                tn = p.e["tensor"].n
                p.wait("vector", eT, tn)
                ev = p.V(lambda e, g4=g4, i=i: e.tensor_copy(
                    l_xn1[:].rearrange("p (c t) -> p c t", c=NDC)[:, g4 * 4:(g4 + 1) * 4, i * 128:(i + 1) * 128],
                    pb[1][:, 0:512].rearrange("p (j t) -> p j t", j=4)))
                p.wait("tensor", eV, ev)
        p.wait("sync", eV, eV.n)
        p.dma("sync", fm_dram_ap(xn1T_d, NDC), l_xn1[:].rearrange("p (c t) -> p c t", c=NDC), stA)
        p.barrier()

        arena.start()
        k_xnb = arena("k_xnb", [128, NDC * TPC], BF16)
        k_kwb = arena("k_kwb", [128, NDC * 128], BF16)
        k_vwb = arena("k_vwb", [128, NDC * 128], BF16)
        k_ko = arena("k_ko", [128, TPC], BF16)
        k_vo = arena("k_vo", [128, NT * 128], BF16)
        p.dma("gpsimd", k_xnb[:].rearrange("p (c t) -> p c t", c=NDC), fm_dram_ap(xn1T_d, NDC), ldC)
        p.dma("gpsimd", k_kwb[:].rearrange("p (c t) -> p c t", c=NDC), fm_dram_ap(kw_d, NDC), ldC)
        p.dma("gpsimd", k_vwb[:].rearrange("p (c t) -> p c t", c=NDC), fm_dram_ap(vw_d, NDC), ldC)
        p.wait("tensor", ldC, ldC.n)
        for th in range(2):
            mmn = None
            for c in range(NDC):
                mmn = p.T(lambda e, th=th, c=c: e.matmul(
                    pb[th][:, :], k_kwb[:, c * 128:(c + 1) * 128],
                    k_xnb[:, c * TPC + th * 512: c * TPC + th * 512 + 512],
                    start=(c == 0), stop=(c == NDC - 1)))
            p.wait("vector", eT, mmn)
            ev = p.V(lambda e, th=th: e.tensor_copy(k_ko[:, th * 512:(th + 1) * 512], pb[th][:, :]))
            p.wait("tensor", eV, ev)
        for i in range(NT):
            mmn = None
            for c in range(NDC):
                mmn = p.T(lambda e, i=i, c=c: e.matmul(
                    pb[2 + i % 2][:, 0:128], k_xnb[:, c * TPC + i * 128: c * TPC + (i + 1) * 128],
                    k_vwb[:, c * 128:(c + 1) * 128], start=(c == 0), stop=(c == NDC - 1)))
            p.wait("vector", eT, mmn)
            ev = p.V(lambda e, i=i: e.tensor_copy(k_vo[:, i * 128:(i + 1) * 128], pb[2 + i % 2][:, 0:128]))
            p.wait("tensor", eV, ev)
        p.wait("sync", eV, eV.n)
        p.dma("sync", kin_t.ap()[:, :], k_ko[:], stA)
        p.dma("sync", vin_t.ap().rearrange("(m p) d -> p m d", p=128), k_vo[:].rearrange("p (m d) -> p m d", d=128), stA)
        p.barrier()
        p.coll("AllGather", [kin_t.ap().opt()], [kg_t.ap().opt()], cc)
        p.coll("AllGather", [vin_t.ap().opt()], [vg_t.ap().opt()], cc)

        arena.start()
        a_xnb = arena("a_xnb", [128, NDC * TPC], BF16)
        a_ksb = arena("a_ksb", [128, SEQ], BF16)
        a_vsb = arena("a_vsb", [128, 32 * 128], BF16)
        a_qsl = [arena("a_qsl", [128, NDC * 128], BF16) for _ in range(2)]
        a_osl = [arena("a_osl", [128, NH * 128], BF16) for _ in range(2)]
        a_qbf = arena("a_qbf", [128, NH * TPC], BF16)
        a_aob = arena("a_aob", [128, NH * TPC], BF16)
        a_q2 = arena("a_q2", [128, TPC])
        a_k2 = arena("a_k2", [128, 512])
        a_kn2 = arena("a_kn2", [1, 512])
        a_km1 = arena("a_km1", [1, 1])
        a_kmax2 = arena("a_kmax2", [1, 1])
        a_qn2 = arena("a_qn2", [1, TPC])
        a_mhat = arena("a_mhat", [1, TPC])
        a_rrow = arena("a_rrow", [1, 512])
        a_rbc = arena("a_rbc", [128, 512])
        a_prs = [arena("a_prs", [128, 512], BF16) for _ in range(2)]
        a_ost = [arena("a_ost", [128, 512]) for _ in range(2)]
        p.dma("gpsimd", a_xnb[:].rearrange("p (c t) -> p c t", c=NDC), fm_dram_ap(xn1T_d, NDC), ldC)
        ldc9 = ldC.n
        p.wait("gpsimd", cc, cc.n)
        for j in range(4):
            p.idma(a_ksb[:, j * TPC:(j + 1) * TPC], kg_t.ap(), kidx[:, j:j + 1], 0, gqR[j])
        for m in range(32):
            p.idma(a_vsb[:, m * 128:(m + 1) * 128], vg_t.ap(), vidx[:, m:m + 1], 0, gqR[m])
        for s in gqR.sems:
            p.wait("tensor", s, s.n)
            p.wait("scalar", s, s.n)
        p.wait("tensor", ldC, ldc9)
        qmm_done = {}
        for oc in range(NH):
            par = oc % 2
            lds = qsR[oc]
            if oc >= 2:
                p.wait("gpsimd", eT, qmm_done[oc - 2])
            ldn = p.dma("gpsimd", a_qsl[par][:].rearrange("p (c i) -> p c i", c=NDC),
                        qw_d[oc].rearrange("c p i -> p c i"), lds)
            p.wait("tensor", lds, ldn)
            for th in range(2):
                mmn = None
                for c in range(NDC):
                    mmn = p.T(lambda e, par=par, c=c, th=th: e.matmul(
                        pb[th][:, :], a_qsl[par][:, c * 128:(c + 1) * 128],
                        a_xnb[:, c * TPC + th * 512: c * TPC + th * 512 + 512],
                        start=(c == 0), stop=(c == NDC - 1)))
                qmm_done[oc] = mmn
                p.wait("vector", eT, mmn)
                ev = p.V(lambda e, oc=oc, th=th: e.tensor_scalar(
                    a_qbf[:, oc * TPC + th * 512: oc * TPC + th * 512 + 512],
                    pb[th][:, :], scl[:], None, ALU.mult))
                p.wait("tensor", eV, ev)
        p.V(lambda e: e.memset(a_kmax2[:], 0.0))
        for j in range(8):
            a1 = p.S(lambda e, j=j: e.activation(a_k2[:], a_ksb[:, j * 512:(j + 1) * 512], AF.Square))
            p.wait("tensor", eS, a1)
            mmn = p.T(lambda e, j=j: e.matmul(
                pbrow[:, :], onescf[:], a_k2[:], start=True, stop=True))
            p.wait("vector", eT, mmn)
            p.V(lambda e: e.tensor_copy(a_kn2[:], pbrow[:, :]))
            p.V(lambda e: e.reduce_max(a_km1[:], a_kn2[:], axis=AX.X))
            ev = p.V(lambda e: e.tensor_tensor(a_kmax2[:], a_kmax2[:], a_km1[:], ALU.max))
            p.wait("tensor", eV, ev)
            p.wait("scalar", eV, ev)
        km_v = p.e["vector"].n
        q_ready = p.e["vector"].n
        p.wait("scalar", eV, q_ready)
        st_i = 0
        store_done = {}
        for h in range(NH):
            a2h = p.S(lambda e, h=h: e.activation(a_q2[:], a_qbf[:, h * TPC:(h + 1) * TPC], AF.Square))
            p.wait("tensor", eS, a2h)
            for th in range(2):
                mmn = p.T(lambda e, th=th: e.matmul(
                    pbrow[:, :], onescf[:], a_q2[:, th * 512:(th + 1) * 512], start=True, stop=True))
                p.wait("vector", eT, mmn)
                ev = p.V(lambda e, th=th: e.tensor_copy(a_qn2[:, th * 512:(th + 1) * 512], pbrow[:, :]))
                p.wait("tensor", eV, ev)
            p.wait("scalar", eV, p.e["vector"].n)
            amh = p.S(lambda e: e.activation(a_mhat[:], a_qn2[:], AF.Sqrt, scale=a_kmax2[:]))
            p.wait("vector", eS, amh)
            nmh = p.V(lambda e: e.tensor_single_scalar(a_mhat[:], a_mhat[:], -1.0, ALU.mult))
            p.wait("tensor", eV, nmh)
            for th in range(2):
                tq = slice(th * 512, (th + 1) * 512)
                for mt in range(32):
                    par = mt % 2
                    p.T(lambda e, h=h, mt=mt, tq=tq, par=par: e.matmul(
                        pb[par][:, :], a_ksb[:, mt * 128:(mt + 1) * 128],
                        a_qbf[:, h * TPC + tq.start: h * TPC + tq.stop], start=True, stop=False))
                    mmb = p.T(lambda e, tq=tq, par=par: e.matmul(
                        pb[par][:, :], ones1f[:], a_mhat[:, tq], start=False, stop=True))
                    p.wait("scalar", eT, mmb)
                    aex = p.S(lambda e, par=par: e.activation(a_prs[par][:], pb[par][:, :], AF.Exp))
                    p.wait("tensor", eS, aex)
                    p.T(lambda e, mt=mt, par=par: e.matmul(
                        pbrow[:, :], onesbf[:], a_prs[par][:], start=(mt == 0), stop=(mt == 31)))
                    p.T(lambda e, mt=mt, par=par: e.matmul(
                        pb[2][:, :], a_vsb[:, mt * 128:(mt + 1) * 128], a_prs[par][:],
                        start=(mt == 0), stop=(mt == 31)))
                avm = p.e["tensor"].n
                p.wait("vector", eT, avm)
                p.V(lambda e: e.reciprocal(a_rrow[:], pbrow[:, :]))
                rv = p.e["vector"].n
                p.wait("tensor", eV, rv)
                bcm = p.T(lambda e: e.matmul(pb[3][:, :], ones1f[:], a_rrow[:], start=True, stop=True))
                p.wait("vector", eT, bcm)
                p.V(lambda e: e.tensor_copy(a_rbc[:], pb[3][:, :]))
                ev = p.V(lambda e, h=h, tq=tq: e.tensor_tensor(
                    a_aob[:, h * TPC + tq.start: h * TPC + tq.stop], pb[2][:, :], a_rbc[:], ALU.mult))
                p.wait("tensor", eV, ev)
        omm_done = {}
        for dc in range(NDC):
            opar = dc % 2
            if dc >= 2:
                p.wait("gpsimd", eT, omm_done[dc - 2])
            ldn = p.dma("gpsimd", a_osl[opar][:].rearrange("p (c i) -> p c i", c=NH),
                        ow_d[dc].rearrange("c p i -> p c i"), oS, serial=True)
            p.wait("tensor", oS, oS.n)
            for th in range(2):
                mmn = None
                for oc in range(NH):
                    mmn = p.T(lambda e, dc=dc, th=th, oc=oc, opar=opar: e.matmul(
                        pb[4 + (dc * 2 + th) % 2][:, :], a_osl[opar][:, oc * 128:(oc + 1) * 128],
                        a_aob[:, oc * TPC + th * 512: oc * TPC + th * 512 + 512],
                        start=(oc == 0), stop=(oc == NH - 1)))
                omm_done[dc] = mmn
                p.wait("vector", eT, mmn)
                if st_i >= 2:
                    p.wait("vector", *store_done[st_i - 2])
                ev = p.V(lambda e, dc=dc, th=th: e.tensor_copy(a_ost[(dc * 2 + th) % 2][:], pb[4 + (dc * 2 + th) % 2][:, :]))
                p.wait("tensor", eV, ev)
                p.wait("sync", eV, ev)
                sts = stR[st_i]
                p.dma("sync", attnT_d[dc * 128:(dc + 1) * 128, th * 512:(th + 1) * 512],
                      a_ost[(dc * 2 + th) % 2][:], sts)
                store_done[st_i] = (sts, sts.n)
                st_i += 1
        p.barrier()

        arena.start()
        acc = arena("q_acc", [128, NDC * TPC])
        keepmark = arena.mark()
        q_xnb = arena("q_xnb", [128, NDC * TPC], BF16)
        q_act = arena("q_act", [128, 8 * TPC], BF16)
        q_upsl = arena("q_upsl", [128, 4 * NDC * 128], BF16)
        q_dnsl = arena("q_dnsl", [128, 8 * 8 * 128], BF16)
        q_rw = arena("q_rw", [128, NDC * 16])
        q_logT = arena("q_logT", [16, TPC])
        q_xs = [arena("q_xs", [128, 512]) for _ in range(2)]
        q_combT = arena("q_combT", [16, TPC])
        q_combbc = arena("q_combbc", [128, TPC])
        q_sct = arena("q_sct", [128, 16])
        q_sbt = arena("q_sbt", [128, 16])
        q_ptmp = arena("q_ptmp", [128, 24])
        q_gs = arena("q_gs", [128, 4])
        q_gs8 = arena("q_gs8", [128, 8])
        q_gsort = arena("q_gsort", [128, 8])
        q_gmask = arena("q_gmask", [128, 4])
        q_mskd = arena("q_mskd", [128, 16])
        q_msk2 = arena("q_msk2", [128, 16])
        q_eqt = arena("q_eqt", [128, 16])
        q_eqi1 = arena("q_eqi1", [128, 16])
        q_eqi2 = arena("q_eqi2", [128, 16])
        q_tt = arena("q_tt", [128, 16])
        q_junk16 = arena("q_junk16", [128, 16])
        q_m1 = arena("q_m1", [128, 1])
        q_i1 = arena("q_i1", [128, 1])
        q_i2 = arena("q_i2", [128, 1])
        q_w1c = arena("q_w1c", [128, 1])
        q_w2c = arena("q_w2c", [128, 1])
        q_den = arena("q_den", [128, 1])
        q_rden = arena("q_rden", [128, 1])
        q_w1n = arena("q_w1n", [128, 1])
        q_w2n = arena("q_w2n", [128, 1])
        q_comb = arena("q_comb", [128, 16])
        q_r = arena("q_r", [128, 512])
        q_sq = arena("q_sq", [128, 512])
        p.dma("gpsimd", q_xnb[:].rearrange("p (c t) -> p c t", c=NDC), fm_dram_ap(xn1T_d, NDC), ldC)
        p.dma("sync", q_rw[:].rearrange("p (c t) -> p c t", c=NDC), fm_dram_ap(rw_d, NDC), ldA)
        ldq0 = ldA.n
        p.wait("tensor", ldA, ldq0)
        vz = p.V(lambda e: e.memset(acc[:], 0.0))
        p.V(lambda e: e.memset(q_gs8[:], -BIG))
        for th in range(2):
            mmn = None
            for c in range(NDC):
                lds = ldR[th * NDC + c]
                ldn = p.dma("sync", q_xs[c % 2][:],
                            xn1T_d[c * 128:(c + 1) * 128, th * 512:(th + 1) * 512], lds)
                p.wait("tensor", lds, ldn)
                mmn = p.T(lambda e, c=c: e.matmul(
                    pb[4][0:16, 0:512], q_rw[:, c * 16:(c + 1) * 16], q_xs[c % 2][:],
                    start=(c == 0), stop=(c == NDC - 1)))
                if c < NDC - 1:
                    p.wait("sync", eT, mmn)
            p.wait("vector", eT, mmn)
            lv = p.V(lambda e, th=th: e.tensor_copy(q_logT[:, th * 512:(th + 1) * 512], pb[4][0:16, 0:512]))
            p.wait("tensor", eV, lv)
        for i in range(NT):
            p.wait("tensor", eV, p.e["vector"].n)
            tr = p.T(lambda e, i=i: e.transpose(pb[5][:, 0:16], q_logT[:, i * 128:(i + 1) * 128], ident[0:16, 0:16]))
            p.wait("scalar", eT, tr)
            sg = p.S(lambda e: e.activation(q_sct[:], pb[5][:, 0:16], AF.Sigmoid))
            p.wait("vector", eS, sg)
            p.wait("tensor", eS, sg)
            p.V(lambda e: e.tensor_tensor(q_sbt[:], q_sct[:], ebias_bc[:], ALU.add))
            for pi, (aa, bb) in enumerate([(0, 1), (0, 2), (0, 3), (1, 2), (1, 3), (2, 3)]):
                p.V(lambda e, aa=aa, bb=bb, pi=pi: e.tensor_tensor(
                    q_ptmp[:, pi * 4:(pi + 1) * 4],
                    q_sbt[:].rearrange("p (g j) -> p g j", j=4)[:, :, aa],
                    q_sbt[:].rearrange("p (g j) -> p g j", j=4)[:, :, bb], ALU.add))
            p.V(lambda e: e.tensor_tensor(q_gs[:], q_ptmp[:, 0:4], q_ptmp[:, 4:8], ALU.max))
            for pi in range(2, 6):
                p.V(lambda e, pi=pi: e.tensor_tensor(q_gs[:], q_gs[:], q_ptmp[:, pi * 4:(pi + 1) * 4], ALU.max))
            p.V(lambda e: e.tensor_copy(q_gs8[:, 0:4], q_gs[:]))
            p.V(lambda e: e.max(out=q_gsort[:], in_=q_gs8[:]))
            p.V(lambda e: e.tensor_scalar(q_gmask[:], q_gs[:], q_gsort[:, 1:2], None, ALU.is_ge))
            p.V(lambda e: e.tensor_tensor(
                q_mskd[:].rearrange("p (g j) -> p g j", j=4),
                q_sbt[:].rearrange("p (g j) -> p g j", j=4),
                q_gmask[:].rearrange("p (g o) -> p g o", o=1).to_broadcast([128, 4, 4]), ALU.mult))
            p.V(lambda e: e.reduce_max(q_m1[:], q_mskd[:], axis=AX.X))
            p.V(lambda e: e.tensor_scalar(q_eqt[:], q_mskd[:], q_m1[:], None, ALU.is_equal))
            p.V(lambda e: e.tensor_tensor(q_tt[:], q_eqt[:], iota16[:], ALU.mult))
            p.V(lambda e: e.scalar_tensor_tensor(q_tt[:], q_eqt[:], -BIG, q_tt[:], ALU.mult, ALU.add))
            p.V(lambda e: e.tensor_single_scalar(q_tt[:], q_tt[:], BIG, ALU.add))
            p.V(lambda e: e.tensor_reduce(q_i1[:], q_tt[:], axis=AX.X, op=ALU.min))
            p.V(lambda e: e.tensor_scalar(q_eqi1[:], iota16[:], q_i1[:], None, ALU.is_equal))
            p.V(lambda e: e.scalar_tensor_tensor(q_junk16[:], q_eqi1[:], 0.0, q_sct[:], ALU.bypass, ALU.mult, accum_out=q_w1c[:]))
            p.V(lambda e: e.scalar_tensor_tensor(q_msk2[:], q_eqi1[:], -BIG, q_mskd[:], ALU.mult, ALU.add))
            p.V(lambda e: e.reduce_max(q_m1[:], q_msk2[:], axis=AX.X))
            p.V(lambda e: e.tensor_scalar(q_eqt[:], q_msk2[:], q_m1[:], None, ALU.is_equal))
            p.V(lambda e: e.tensor_tensor(q_tt[:], q_eqt[:], iota16[:], ALU.mult))
            p.V(lambda e: e.scalar_tensor_tensor(q_tt[:], q_eqt[:], -BIG, q_tt[:], ALU.mult, ALU.add))
            p.V(lambda e: e.tensor_single_scalar(q_tt[:], q_tt[:], BIG, ALU.add))
            p.V(lambda e: e.tensor_reduce(q_i2[:], q_tt[:], axis=AX.X, op=ALU.min))
            p.V(lambda e: e.tensor_scalar(q_eqi2[:], iota16[:], q_i2[:], None, ALU.is_equal))
            p.V(lambda e: e.scalar_tensor_tensor(q_junk16[:], q_eqi2[:], 0.0, q_sct[:], ALU.bypass, ALU.mult, accum_out=q_w2c[:]))
            p.V(lambda e: e.tensor_tensor(q_den[:], q_w1c[:], q_w2c[:], ALU.add))
            p.V(lambda e: e.tensor_single_scalar(q_den[:], q_den[:], 1e-20, ALU.add))
            p.V(lambda e: e.reciprocal(q_rden[:], q_den[:]))
            p.V(lambda e: e.tensor_tensor(q_w1n[:], q_w1c[:], q_rden[:], ALU.mult))
            p.V(lambda e: e.tensor_tensor(q_w2n[:], q_w2c[:], q_rden[:], ALU.mult))
            p.V(lambda e: e.tensor_scalar(q_comb[:], q_eqi1[:], q_w1n[:], None, ALU.mult))
            cv = p.V(lambda e: e.scalar_tensor_tensor(q_comb[:], q_eqi2[:], q_w2n[:], q_comb[:], ALU.mult, ALU.add))
            p.wait("tensor", eV, cv)
            tr2 = p.T(lambda e: e.transpose(pb[5][0:16, 0:128], q_comb[:], ident[:]))
            p.wait("vector", eT, tr2)
            p.V(lambda e, i=i: e.tensor_copy(q_combT[:, i * 128:(i + 1) * 128], pb[5][0:16, 0:128]))
        comb_v = p.e["vector"].n
        p.wait("tensor", eV, comb_v)
        p.wait("tensor", ldC, ldC.n)
        wl = 0
        upsl_free = None
        dnsl_free = None
        for ex in range(16):
            for th in range(2):
                bmn = p.T(lambda e, ex=ex, th=th: e.matmul(
                    pb[6][:, :], esel[:, ex * 128:(ex + 1) * 128],
                    q_combT[:, th * 512:(th + 1) * 512], start=True, stop=True))
                p.wait("vector", eT, bmn)
                cbv = p.V(lambda e, th=th: e.tensor_copy(q_combbc[:, th * 512:(th + 1) * 512], pb[6][:, :]))
                p.wait("tensor", eV, cbv)
            for ih in range(2):
                lds = upS
                if upsl_free is not None:
                    p.wait("gpsimd", eV, upsl_free)
                ldn = p.dma("gpsimd", q_upsl[:].rearrange("p (a c i) -> p a c i", a=4, c=NDC),
                            wup_d[ex, ih * 4:(ih + 1) * 4].rearrange("a c p i -> p a c i"), lds, serial=True)
                p.wait("tensor", lds, ldn)
                for ic4 in range(4):
                    ic = ih * 4 + ic4
                    for th in range(2):
                        par = (ic * 2 + th) % 2
                        mmn = None
                        for dc in range(NDC):
                            mmn = p.T(lambda e, ic4=ic4, dc=dc, th=th, par=par: e.matmul(
                                pb[par][:, :], q_upsl[:, (ic4 * NDC + dc) * 128:(ic4 * NDC + dc + 1) * 128],
                                q_xnb[:, dc * TPC + th * 512: dc * TPC + th * 512 + 512],
                                start=(dc == 0), stop=(dc == NDC - 1)))
                        p.wait("scalar", eT, mmn)
                        rl = p.S(lambda e, par=par: e.activation(q_r[:], pb[par][:, :], AF.Relu))
                        sq = p.S(lambda e: e.activation(q_sq[:], q_r[:], AF.Square))
                        p.wait("vector", eS, sq)
                        av = p.V(lambda e, ic=ic, th=th: e.tensor_tensor(
                            q_act[:, ic * TPC + th * 512: ic * TPC + th * 512 + 512],
                            q_sq[:], q_combbc[:, th * 512:(th + 1) * 512], ALU.mult))
                        p.wait("tensor", eV, av)
                upsl_free = p.e["vector"].n
            for dh in range(2):
                lds = dnS
                if dnsl_free is not None:
                    p.wait("gpsimd", eT, dnsl_free)
                ldn = p.dma("gpsimd", q_dnsl[:].rearrange("p (a c i) -> p a c i", a=8, c=8),
                            wdn_d[ex, dh * 8:(dh + 1) * 8].rearrange("a c p i -> p a c i"), lds, serial=True)
                p.wait("tensor", lds, ldn)
                for dc8 in range(8):
                    dc = dh * 8 + dc8
                    for th in range(2):
                        par = 2 + (dc * 2 + th) % 2
                        mmn = None
                        for ic in range(8):
                            mmn = p.T(lambda e, dc8=dc8, ic=ic, th=th, par=par: e.matmul(
                                pb[par][:, :], q_dnsl[:, (dc8 * 8 + ic) * 128:(dc8 * 8 + ic + 1) * 128],
                                q_act[:, ic * TPC + th * 512: ic * TPC + th * 512 + 512],
                                start=(ic == 0), stop=(ic == 7)))
                        p.wait("vector", eT, mmn)
                        avv = p.V(lambda e, dc=dc, th=th, par=par: e.tensor_tensor(
                            acc[:, dc * TPC + th * 512: dc * TPC + th * 512 + 512],
                            pb[par][:, :],
                            acc[:, dc * TPC + th * 512: dc * TPC + th * 512 + 512], ALU.add))
                        p.wait("tensor", eV, avv)
                dnsl_free = p.e["tensor"].n
        for tq4 in range(4):
            qsl_ = slice(tq4 * 256, (tq4 + 1) * 256)
            for ih in range(8):
                lds = upS
                if upsl_free is not None:
                    p.wait("gpsimd", eV, upsl_free)
                ldn = p.dma("gpsimd", q_upsl[:].rearrange("p (a c i) -> p a c i", a=4, c=NDC),
                            shu_d[ih * 4:(ih + 1) * 4].rearrange("a c p i -> p a c i"), lds, serial=True)
                p.wait("tensor", lds, ldn)
                for ic4 in range(4):
                    ic = ih * 4 + ic4
                    par = ic % 2
                    mmn = None
                    for dc in range(NDC):
                        mmn = p.T(lambda e, ic4=ic4, dc=dc, par=par, qsl_=qsl_: e.matmul(
                            pb[par][:, 0:256], q_upsl[:, (ic4 * NDC + dc) * 128:(ic4 * NDC + dc + 1) * 128],
                            q_xnb[:, dc * TPC + qsl_.start: dc * TPC + qsl_.stop],
                            start=(dc == 0), stop=(dc == NDC - 1)))
                    p.wait("scalar", eT, mmn)
                    rl = p.S(lambda e, par=par: e.activation(q_r[:, 0:256], pb[par][:, 0:256], AF.Relu))
                    sq = p.S(lambda e: e.activation(q_sq[:, 0:256], q_r[:, 0:256], AF.Square))
                    p.wait("vector", eS, sq)
                    av = p.V(lambda e, ic=ic: e.tensor_copy(q_act[:, ic * 256:(ic + 1) * 256], q_sq[:, 0:256]))
                    p.wait("tensor", eV, av)
                upsl_free = p.e["vector"].n
            for dc2 in range(8):
                lds = dnS
                if dnsl_free is not None:
                    p.wait("gpsimd", eT, dnsl_free)
                ldn = p.dma("gpsimd", q_dnsl[:, 0:2 * 32 * 128].rearrange("p (a c i) -> p a c i", a=2, c=32),
                            shd_d[dc2 * 2:(dc2 + 1) * 2].rearrange("a c p i -> p a c i"), lds, serial=True)
                p.wait("tensor", lds, ldn)
                for j in range(2):
                    dc = dc2 * 2 + j
                    par = 2 + dc % 2
                    mmn = None
                    for ic in range(32):
                        mmn = p.T(lambda e, j=j, ic=ic, par=par: e.matmul(
                            pb[par][:, 0:256], q_dnsl[:, (j * 32 + ic) * 128:(j * 32 + ic + 1) * 128],
                            q_act[:, ic * 256:(ic + 1) * 256], start=(ic == 0), stop=(ic == 31)))
                    p.wait("vector", eT, mmn)
                    avv = p.V(lambda e, dc=dc, tq4=tq4, par=par: e.tensor_tensor(
                        acc[:, dc * TPC + tq4 * 256: dc * TPC + (tq4 + 1) * 256],
                        pb[par][:, 0:256],
                        acc[:, dc * TPC + tq4 * 256: dc * TPC + (tq4 + 1) * 256], ALU.add))
                    p.wait("tensor", eV, avv)
                dnsl_free = p.e["tensor"].n
        p.barrier()
        fi = 0
        fstore = {}
        f_x1 = q_xs
        f_at = [q_r, q_sq]
        f_os = [arena("f_os", [128, 512]) for _ in range(2)]
        for dc in range(NDC):
            for th in range(2):
                par = fi % 2
                lds = ldR[fi]
                csl = slice(dc * TPC + th * 512, dc * TPC + th * 512 + 512)
                r0_, c0_ = slice(dc * 128, (dc + 1) * 128), slice(th * 512, (th + 1) * 512)
                if fi >= 2:
                    p.wait("sync", *fstore[fi - 2])
                p.dma("sync", f_x1[par][:], x1T_d[r0_, c0_], lds)
                ldn = p.dma("sync", f_at[par][:], attnT_d[r0_, c0_], lds)
                p.wait("vector", lds, ldn)
                p.V(lambda e, par=par, csl=csl: e.tensor_tensor(f_os[par][:], acc[:, csl], f_x1[par][:], ALU.add))
                fv = p.V(lambda e, par=par: e.tensor_tensor(f_os[par][:], f_os[par][:], f_at[par][:], ALU.add))
                p.wait("sync", eV, fv)
                sts = stR[fi]
                p.dma("sync", outT_d[r0_, c0_], f_os[par][:], sts)
                fstore[fi] = (sts, sts.n)
                fi += 1
        p.barrier()

        arena.start()
        o_img = arena("o_img", [128, NDC * TPC])
        o_tt = [arena("o_tt", [128, DM]) for _ in range(2)]
        fm_load(p, o_img, outT_d, NDC, ldA)
        p.wait("tensor", ldA, ldA.n)
        ostore = {}
        for i in range(NT):
            par = i % 2
            if i >= 2:
                p.wait("vector", *ostore[i - 2])
            for g4 in range(4):
                p.wait("tensor", eV, p.e["vector"].n)
                for j in range(4):
                    cch = g4 * 4 + j
                    p.T(lambda e, cch=cch, j=j, i=i: e.transpose(
                        pb[0][:, j * 128:(j + 1) * 128],
                        o_img[:, cch * TPC + i * 128: cch * TPC + (i + 1) * 128], ident[:]))
                tn = p.e["tensor"].n
                p.wait("vector", eT, tn)
                p.V(lambda e, g4=g4, par=par: e.tensor_copy(
                    o_tt[par][:, g4 * 512:(g4 + 1) * 512], pb[0][:, 0:512]))
            ov = p.e["vector"].n
            p.wait("sync", eV, ov)
            p.dma("sync", out_d[i * 128:(i + 1) * 128, :], o_tt[par][:], stR[i])
            ostore[i] = (stR[i], stR[i].n)
        p.barrier()
        p.finish()
    return nc


def stage_fused_inputs(inputs):
    C = np.ascontiguousarray
    f32 = np.float32
    x = C(np.asarray(inputs["L0_attn_x"], f32).reshape(NCORES * TPC, DM))
    W_inT = C(np.asarray(inputs["L0_attn_W_in"], f32).T)
    W_dtT = C(np.asarray(inputs["L0_attn_W_dt"], f32).T)
    b_dt_r = C(np.asarray(inputs["L0_attn_b_dt"], f32).reshape(16, 128).T)
    A_log = np.asarray(inputs["L0_attn_A_log"], f32)
    A_log_r = C(A_log.reshape(16, 128, 16).transpose(1, 0, 2).reshape(128, 256))
    D_r = C(np.asarray(inputs["L0_attn_D"], f32).reshape(16, 128).T)
    W_q = C(np.asarray(inputs["L0_ffn_W_q"], f32))
    keys = np.asarray(inputs["L0_ffn_keys"], f32)
    gate_r = C(np.asarray(inputs["L0_ffn_gate_w"], f32).reshape(16, 128, 64, 128).transpose(2, 0, 1, 3))
    up_r = C(np.asarray(inputs["L0_ffn_up_w"], f32).reshape(16, 128, 64, 128).transpose(2, 0, 1, 3))
    down_r = C(np.asarray(inputs["L0_ffn_down_w"], f32).reshape(64, 128, 16, 128).transpose(2, 0, 1, 3))
    de_w = C(np.asarray(inputs["L0_ffn_down_embed"], f32))
    ue_w = C(np.asarray(inputs["L0_ffn_up_embed"], f32))
    k_wT = C(np.asarray(inputs["L1_attn_k_w"], f32).T)
    v_wT = C(np.asarray(inputs["L1_attn_v_w"], f32).T)
    qw_r = C(np.asarray(inputs["L1_attn_q_w"], f32).T.reshape(16, 128, 16, 128).transpose(2, 0, 1, 3))
    ow_r = C(np.asarray(inputs["L1_attn_o_w"], f32).T.reshape(16, 128, 16, 128).transpose(2, 0, 1, 3))
    rwT = C(np.asarray(inputs["L1_ffn_router_weight"], f32).T)
    ebias = C(np.asarray(inputs["L1_ffn_e_bias"], f32)[None, :])
    w_up_r = C(np.asarray(inputs["L1_ffn_w_up"], f32).reshape(16, 16, 128, 8, 128).transpose(0, 3, 1, 2, 4))
    w_down_r = C(np.asarray(inputs["L1_ffn_w_down"], f32).reshape(16, 8, 128, 16, 128).transpose(0, 3, 1, 2, 4))
    shup_r = C(np.asarray(inputs["L1_ffn_shared_up"], f32).reshape(16, 128, 32, 128).transpose(2, 0, 1, 3))
    shdown_r = C(np.asarray(inputs["L1_ffn_shared_down"], f32).reshape(32, 128, 16, 128).transpose(2, 0, 1, 3))

    prng = np.arange(128, dtype=np.int64)
    maps = []
    for c in range(NCORES):
        b = c // 4
        k = c % 4
        alpha = np.zeros((128, 8), f32)
        for j in range(b * 4, b * 4 + k):
            alpha[:, j] = 1.0
        pidx0 = np.zeros((128, NT), np.int32)
        pidx1 = np.zeros((128, NT), np.int32)
        for i in range(NT):
            pidx0[:, i] = 1024 * c + i * 128 + prng
            pidx1[:, i] = 8192 + 1024 * c + i * 128 + prng
        kidx = np.zeros((128, 4), np.int32)
        for j in range(4):
            kidx[:, j] = 512 * b + j * 128 + prng
        vidx = np.zeros((128, 32), np.int32)
        for m in range(32):
            vidx[:, m] = 4096 * b + m * 128 + prng
        maps.append(dict(
            x_tok=C(x[c * TPC:(c + 1) * TPC]),
            W_inT=W_inT, W_dtT=W_dtT, b_dt_r=b_dt_r, A_log_r=A_log_r, D_r=D_r,
            W_q=W_q, keys_own=C(keys[b]), alpha=alpha,
            gate_r=gate_r, up_r=up_r, down_r=down_r,
            down_embed=de_w, up_embed=ue_w, pidx0=pidx0, pidx1=pidx1,
            k_wT=k_wT, v_wT=v_wT, qw_r=qw_r, ow_r=ow_r, kidx=kidx, vidx=vidx,
            router_wT=rwT, e_bias=ebias, w_up_r=w_up_r, w_down_r=w_down_r,
            shup_r=shup_r, shdown_r=shdown_r,
        ))
    return maps


def kernel(**inputs):
    maps = stage_fused_inputs(inputs)
    nc = build_fused()
    res = run_bass_kernel_spmd(nc, maps, core_ids=list(range(NCORES))).results
    out = np.concatenate([r["out"] for r in res], axis=0)
    return np.ascontiguousarray(out.reshape(2, SEQ, DM)).astype(np.float32)
